# version 08
# baseline (speedup 1.0000x reference)
import numpy as np
import ml_dtypes
from contextlib import ExitStack

import concourse.bass as bass
import concourse.mybir as mybir
from concourse.bass_utils import run_bass_kernel_spmd

F32 = mybir.dt.float32
BF16 = mybir.dt.bfloat16
I32 = mybir.dt.int32
AF = mybir.ActivationFunctionType
ALU = mybir.AluOpType
AX = mybir.AxisListType

S = 4096
D = 1024
NT = S // 128
NC = D // 128
INNER = 4096
H0, HD0 = 16, 64
H1, HKV1, HD1 = 8, 4, 128
L0_EXP_SCALE = 0.125
L0_EXP_BIAS = -30.0
L1_EXP_SCALE = 0.08838834764831845
EPS = 1e-5

DEBUG_TAPS = ()


class KB:

    ENGS = ("T", "V", "S", "Y", "G")

    def __init__(self, nc, es):
        self.nc = nc
        self.es = es
        self.block = es.enter_context(nc.Block())
        self.sems = {}
        self.cnt = {}
        self.track = {}
        self.lastw = {}
        self.queue = {e: [] for e in self.ENGS}
        self.phase_tokens = {}
        for e in ("T", "V", "S"):
            self._mk_sem("c" + e)

    def _mk_sem(self, name):
        if name not in self.sems:
            self.sems[name] = self.es.enter_context(self.nc.semaphore(name))
            self.cnt[name] = 0
        return self.sems[name]

    def start_phase(self):
        self.phase_tokens = {s: v for s, v in self.cnt.items() if v > 0}

    def _get(self, k):
        t = self.track.get(k)
        if t is None:
            t = {"w": dict(self.phase_tokens), "r": {}}
            self.track[k] = t
        return t

    def _need(self, reads, writes):
        need = {}

        def add(tok):
            for s, v in tok.items():
                if s[0] == "d":
                    v = self.cnt[s]
                if need.get(s, 0) < v:
                    need[s] = v

        for k in reads:
            t = self._get(k)
            add(t["w"])
        for k in writes:
            t = self._get(k)
            add(t["w"])
            add(t["r"])
        return need

    def _wait(self, eng, need):
        for s, v in sorted(need.items()):
            if self.lastw.get((eng, s), -1) >= v:
                continue
            self.queue[eng].append(("wait", s, v))
            self.lastw[(eng, s)] = v

    def _update(self, reads, writes, sem, val):
        for k in writes:
            self.track[k] = {"w": {sem: val}, "r": {}}
        for k in reads:
            t = self.track.setdefault(k, {"w": {}, "r": {}})
            if t["r"].get(sem, 0) < val:
                t["r"][sem] = val

    def op(self, eng, fn, reads=(), writes=()):
        sem = "c" + eng
        self._wait(eng, self._need(reads, writes))
        self.cnt[sem] += 1
        val = self.cnt[sem]
        self.queue[eng].append(("inst", fn, sem, 1))
        self._update(reads, writes, sem, val)
        return (sem, val)

    def dma(self, eng, sem_name, fn, reads=(), writes=()):
        self._mk_sem(sem_name)
        self._wait(eng, self._need(reads, writes))
        self.cnt[sem_name] += 16
        val = self.cnt[sem_name]
        self.queue[eng].append(("inst", fn, sem_name, 16))
        self._update(reads, writes, sem_name, val)
        return (sem_name, val)

    def wait_token(self, eng, sem_name, val=None):
        if sem_name not in self.cnt:
            return
        v = self.cnt[sem_name] if val is None else val
        if self.lastw.get((eng, sem_name), -1) >= v:
            return
        self.queue[eng].append(("wait", sem_name, v))
        self.lastw[(eng, sem_name)] = v

    def barrier_key(self, key):
        self.track[key] = {"w": {s: v for s, v in self.cnt.items() if v > 0}, "r": {}}

    def flush(self):
        engmap = {e: q for e, q in self.queue.items() if q}
        self.queue = {e: [] for e in self.ENGS}

        def run(engine, items):
            for it in items:
                if it[0] == "wait":
                    engine.wait_ge(self.sems[it[1]], it[2])
                else:
                    _, fn, sem, inc = it
                    fn(engine).then_inc(self.sems[sem], inc)

        block = self.block
        if "Y" in engmap:
            @block.sync
            def _(sync, items=engmap["Y"]):
                run(sync, items)
        if "G" in engmap:
            @block.gpsimd
            def _(gpsimd, items=engmap["G"]):
                run(gpsimd, items)
        if "T" in engmap:
            @block.tensor
            def _(tensor, items=engmap["T"]):
                run(tensor, items)
        if "V" in engmap:
            @block.vector
            def _(vector, items=engmap["V"]):
                run(vector, items)
        if "S" in engmap:
            @block.scalar
            def _(scalar, items=engmap["S"]):
                run(scalar, items)


class Arena:
    def __init__(self, handle, cols):
        self.h = handle
        self.cols = cols
        self.off = 0
        self.marks = []

    def take(self, ncols, align=32):
        o = (self.off + align - 1) // align * align
        assert o + ncols <= self.cols, f"arena overflow {o}+{ncols}>{self.cols}"
        self.off = o + ncols
        return o

    def push(self):
        self.marks.append(self.off)

    def pop(self):
        self.off = self.marks.pop()


class Builder:
    def __init__(self, debug_taps=(), phases=None, dbg_raw=""):
        self.phases = phases
        self.dbg_raw = dbg_raw
        self.debug_taps = debug_taps
        self.nc = bass.Bass("TRN2", target_bir_lowering=False)
        self.dram = {}

    def din(self, name, shape, dt):
        ap = self.nc.dram_tensor(name, list(shape), dt, kind="ExternalInput").ap()
        self.dram[name] = ap
        return ap

    def dscr(self, name, shape, dt):
        kind = "ExternalOutput" if name in self.debug_taps else "Internal"
        ap = self.nc.dram_tensor(name, list(shape), dt, kind=kind).ap()
        self.dram[name] = ap
        return ap

    def sb(self, name, shape, dt=F32):
        return self.es.enter_context(self.nc.sbuf_tensor(name, list(shape), dt))

    def build(self):
        nc = self.nc
        self.x0_h = self.din("x0", (NT, 128, D), F32)
        self.wqkvT_h = self.din("wqkvT", (D, 3 * D), F32)
        self.woT_h = self.din("woT", (D, D), F32)
        self.wgT_h = self.din("wgT", (D, INNER), F32)
        self.wuT_h = self.din("wuT", (D, INNER), F32)
        self.wdT_h = self.din("wdT", (INNER, D), F32)
        self.wq1_h = self.din("wq1", (D, D), F32)
        self.wk1_h = self.din("wk1", (D, 512), F32)
        self.wv1_h = self.din("wv1", (D, 512), F32)
        self.wo1_h = self.din("wo1", (D, D), F32)
        self.ternT_h = self.din("ternT", (D, D), I32)
        self.bias_h = self.din("bias1", (1, D), F32)
        self.ascale_h = self.din("ascale", (1, 1), F32)
        self.ident_h = self.din("identbf", (128, 128), BF16)
        self.onesf_col_h = self.din("onesf_col", (128, 1), F32)
        self.onesf_row_h = self.din("onesf_row", (1, 128), F32)
        self.onesb_row_h = self.din("onesb_row", (1, 128), BF16)
        self.triu_h = self.din("triubf", (128, 128), BF16)
        self.cosq_h = self.din("cosq", (S, 128), F32)
        self.sinq_h = self.din("sinq", (S, 128), F32)
        self.econst_h = self.din("econsts", (1, 8), F32)

        self.y_h = nc.dram_tensor("y", [NT, 128, D], F32, kind="ExternalOutput").ap()
        self.dram["y"] = self.y_h

        self.q0_hbm = self.dscr("q0s", (NC, NT, 128, 128), BF16)
        self.k0_hbm = self.dscr("k0s", (NC, NT, 128, 128), BF16)
        self.xi_hbm = self.dscr("xis", (NT, 128, D), BF16)
        self.attn0_hbm = self.dscr("attn0s", (NC, NT, 128, 128), BF16)
        self.tg_hbm = self.dscr("tgs", (NC, 4, 128, 1024), BF16)
        self.tu_hbm = self.dscr("tus", (NC, 4, 128, 1024), BF16)
        self.td_hbm = self.dscr("tds", (INNER // 128, 128, D), BF16)
        self.xi2_hbm = self.dscr("xi2s", (NT, 4, 128, 1024), BF16)
        self.x1_hbm = self.dscr("x1s", (NT, 128, D), F32)
        self.qr1_hbm = self.dscr("qr1s", (H1, NT, 128, 128), BF16)
        self.xint_hbm = self.dscr("xints", (NT, 128, D), BF16)
        self.attn1_hbm = self.dscr("attn1s", (H1, NT, 128, 128), BF16)
        self.bc_hbm = self.dscr("bcs", (1, 16), F32)
        self.vdbg_hbm = self.dscr("v0dbg", (NT, 128, H0 * 65), BF16)
        self.g0dbg_hbm = self.dscr("g0dbg", (NT, 128, D), F32)
        self.s0dbg_hbm = self.dscr("s0dbg", (NT, 128, D), F32)

        es = ExitStack()
        self.es = es
        with es:
            self.kb = KB(nc, es)
            kb = self.kb

            AR2C, AR4C = 75296, 4608
            self.ar2h = self.sb("AR2", (128, AR2C), BF16)
            self.ar4h = self.sb("AR4", (128, AR4C), F32)
            self.ar2 = Arena(self.ar2h, AR2C)
            self.ar4 = Arena(self.ar4h, AR4C)
            self.rnd_i = self.sb("rndi", (128, 1024), I32)
            self.stats = self.sb("stats", (128, 32), F32)
            self.consts = self.sb("econsts_sb", (128, 8), F32)
            self.bcsb = self.sb("bcsb", (128, 16), F32)
            self.ident = self.sb("ident_sb", (128, 128), BF16)
            self.onesf_col = self.sb("onesf_col_sb", (128, 1), F32)
            self.onesf_row = self.sb("onesf_row_sb", (1, 128), F32)
            self.onesb_row = self.sb("onesb_row_sb", (1, 128), BF16)
            self.triu = self.sb("triu_sb", (128, 128), BF16)
            self.rl = self.sb("rl", (1, 1056), F32)
            self.b1row = self.sb("b1row", (1, D), F32)
            self.b1bf = self.sb("b1bf", (1, D), BF16)
            self.dqa = self.sb("dqa", (128, NT), F32)
            self.dq2a = self.sb("dq2a", (128, NT), F32)
            self.xt0 = self.sb("xt0", (128, 1024), F32)
            self.xt1 = self.sb("xt1", (128, 1024), F32)
            self.xn_f = self.sb("xnf_s", (128, 1024), F32)
            self.sq_f = self.sb("sqf_s", (128, 1024), F32)
            self.ga_f = self.sb("gaf_s", (128, 1024), F32)
            self.gb_f = self.sb("gbf_s", (128, 1024), F32)

            kb.dma("Y", "dcn", lambda e: e.dma_start(out=self.ident[:], in_=self.ident_h), writes=("ident",))
            kb.dma("Y", "dcn", lambda e: e.dma_start(out=self.onesf_col[:], in_=self.onesf_col_h), writes=("onesc",))
            kb.dma("Y", "dcn", lambda e: e.dma_start(out=self.onesf_row[:], in_=self.onesf_row_h), writes=("onesr",))
            kb.dma("Y", "dcn", lambda e: e.dma_start(out=self.onesb_row[:], in_=self.onesb_row_h), writes=("onesbr",))
            kb.dma("Y", "dcn", lambda e: e.dma_start(out=self.triu[:], in_=self.triu_h), writes=("triu",))
            kb.dma("Y", "dcn", lambda e: e.dma_start(
                out=self.consts[:], in_=self.econst_h[0:1, :].partition_broadcast(128)), writes=("econ",))
            kb.dma("Y", "dcn", lambda e: e.dma_start(out=self.b1row[:], in_=self.bias_h), writes=("b1row",))

            self.o_vaug = self.ar2.take(NT * H0 * 65)

            allp = ["p0l0", "pAl0", "pCl0", "pDl0", "p0l1", "pAl1", "pCl1"]
            want = self.phases if self.phases is not None else allp
            if "p0l0" in want: self.phase0_l0()
            if "pAl0" in want: self.phaseA_l0()
            self.ar2.off = self.o_vaug
            self.ar2.off = 0
            if "pCl0" in want: self.phaseC_l0()
            if "pDl0" in want: self.phaseD_l0()
            if "p0l1" in want: self.phase0_l1()
            if "pAl1" in want: self.phaseA_l1()
            if "pCl1" in want: self.phaseC_l1()

            for s in sorted(kb.cnt):
                if s[0] == "d":
                    kb.wait_token("Y", s)
            kb.flush()
        return nc

    def xt(self, p):
        return (self.xt0, self.xt1)[p]

    def emit_round(self, src_ap, rr_ap, out_ap, src_keys, dst_key, clip=None, n=1024,
                   sc=None):
        kb = self.kb
        ci = self.rnd_i[:, 0:n]
        dd = self.sq_f[:, 0:n]
        if sc is None:
            ga, gb, ka, kb_ = self.ga_f[:, 0:n], self.gb_f[:, 0:n], "rn_a", "rn_b"
        else:
            ga, gb, ka, kb_ = sc
        src_keys = tuple(src_keys)
        kb.op("S", lambda e: e.activation(ci, src_ap, AF.Identity, scale=rr_ap),
              reads=src_keys, writes=("rn_ci",))
        kb.op("V", lambda e: e.scalar_tensor_tensor(dd, src_ap, rr_ap, ci, ALU.mult, ALU.subtract),
              reads=src_keys + ("rn_ci",), writes=("rn_d",))
        kb.op("S", lambda e: e.activation(ga, dd, AF.Sign, bias=self.consts[:, 3:4]),
              reads=("rn_d", "econ"), writes=(ka,))
        kb.op("S", lambda e: e.activation(gb, dd, AF.Sign, bias=self.consts[:, 4:5]),
              reads=("rn_d", "econ"), writes=(kb_,))
        kb.op("V", lambda e: e.tensor_tensor(ga, ga, gb, ALU.add),
              reads=(ka, kb_), writes=(ka,))
        if clip is None:
            kb.op("V", lambda e: e.scalar_tensor_tensor(out_ap, ga, 0.5, ci, ALU.mult, ALU.add),
                  reads=(ka, "rn_ci"), writes=(dst_key,))
        else:
            kb.op("V", lambda e: e.scalar_tensor_tensor(dd, ga, 0.5, ci, ALU.mult, ALU.add),
                  reads=(ka, "rn_ci"), writes=("rn_d",))
            kb.op("V", lambda e: e.tensor_scalar(out_ap, dd, clip, -clip, ALU.min, ALU.max),
                  reads=("rn_d",), writes=(dst_key,))

    def emit_ln(self, x_ap, xkey, out_f32_ap, out_key):
        kb, st = self.kb, self.stats
        kb.op("V", lambda e: e.reduce_sum(st[:, 0:1], x_ap, axis=AX.X),
              reads=(xkey,), writes=("st_sum",))
        kb.op("S", lambda e: e.activation(self.sq_f[:], x_ap, AF.Square, accum_out=st[:, 1:2]),
              reads=(xkey,), writes=("rn_d", "st_sumsq"))
        kb.op("V", lambda e: e.tensor_single_scalar(st[:, 2:3], st[:, 0:1], 1.0 / D, ALU.mult),
              reads=("st_sum",), writes=("st_m",))
        kb.op("V", lambda e: e.tensor_single_scalar(st[:, 3:4], st[:, 1:2], 1.0 / D, ALU.mult),
              reads=("st_sumsq",), writes=("st_t",))
        kb.op("V", lambda e: e.tensor_tensor(st[:, 4:5], st[:, 2:3], st[:, 2:3], ALU.mult),
              reads=("st_m",), writes=("st_m2",))
        kb.op("V", lambda e: e.tensor_tensor(st[:, 5:6], st[:, 3:4], st[:, 4:5], ALU.subtract),
              reads=("st_t", "st_m2"), writes=("st_v",))
        kb.op("V", lambda e: e.tensor_scalar_add(st[:, 6:7], st[:, 5:6], EPS),
              reads=("st_v",), writes=("st_ve",))
        kb.op("V", lambda e: e.reciprocal(st[:, 7:8], st[:, 6:7]),
              reads=("st_ve",), writes=("st_rv",))
        kb.op("S", lambda e: e.activation(st[:, 8:9], st[:, 7:8], AF.Sqrt),
              reads=("st_rv",), writes=("st_r",))
        kb.op("V", lambda e: e.tensor_scalar(out_f32_ap, x_ap, st[:, 2:3], st[:, 8:9],
                                             ALU.subtract, ALU.mult),
              reads=(xkey, "st_m", "st_r"), writes=(out_key,))

    def emit_transpose(self, src_bf_ap, src_keys, dst_ap_fn, dst_key, nblk, evac_eng="V"):
        kb = self.kb
        for cb in range(nblk):
            half = cb % 2
            pslc = self.p_tr[:, half * 1024: half * 1024 + 128]
            src_slc = src_bf_ap[:, cb * 128:(cb + 1) * 128]
            dst_slc = dst_ap_fn(cb)
            kb.op("T", lambda e, src_slc=src_slc, pslc=pslc: e.transpose(
                pslc, src_slc, self.ident[:]),
                reads=tuple(src_keys) + ("ident",), writes=(f"ptr{half}",))
            if evac_eng == "V":
                kb.op("V", lambda e, dst_slc=dst_slc, pslc=pslc: e.tensor_copy(dst_slc, pslc),
                      reads=(f"ptr{half}",), writes=(dst_key,))
            else:
                kb.op("S", lambda e, dst_slc=dst_slc, pslc=pslc: e.activation(
                    dst_slc, pslc, AF.Identity),
                    reads=(f"ptr{half}",), writes=(dst_key,))

    def bcast_scalar(self, src_ap, src_keys, slot, dst_key):
        kb = self.kb
        kb.dma("G", "dbc", lambda e: e.dma_start(
            out=self.bc_hbm[0:1, slot:slot + 1], in_=src_ap),
            reads=tuple(src_keys), writes=(f"bch{slot}",))
        kb.dma("G", "dbc", lambda e: e.dma_start(
            out=self.bcsb[:, slot:slot + 1],
            in_=self.bc_hbm[0:1, slot:slot + 1].partition_broadcast(128)),
            reads=(f"bch{slot}",), writes=(dst_key,))

    def phase0_l0(self):
        kb, ar2, nc = self.kb, self.ar2, self.nc
        kb.start_phase()
        ar2.push()
        ar2h = self.ar2h
        o_wqkv = ar2.take(3 * D * NC)
        o_xnf = ar2.take(2 * NC * 512)
        o_xnb = ar2.take(1024)
        o_xib = ar2.take(1024)
        o_xist = ar2.take(1024)
        o_qkst = ar2.take(2 * 512)
        wqkv = lambda c, lo, hi: ar2h[:, o_wqkv + c * 3072 + lo: o_wqkv + c * 3072 + hi]
        xn_b = ar2h[:, o_xnb: o_xnb + 1024]
        xi_b = ar2h[:, o_xib: o_xib + 1024]
        xist = ar2h[:, o_xist: o_xist + 1024]
        qkst = lambda p: ar2h[:, o_qkst + p * 512: o_qkst + (p + 1) * 512]

        for c in range(NC):
            kb.dma("G", "dwq", lambda e, c=c: e.dma_start(
                out=wqkv(c, 0, 3072), in_=self.wqkvT_h[c * 128:(c + 1) * 128, :]),
                writes=(f"wqkv{c}",))

        with ExitStack() as ps:
            p_qk = ps.enter_context(nc.psum_tensor("p0_qk", [128, 1024], F32))
            p_v0 = ps.enter_context(nc.psum_tensor("p0_v0", [128, 1024], F32))
            p_v1 = ps.enter_context(nc.psum_tensor("p0_v1", [128, 1024], F32))
            self.p_tr = ps.enter_context(nc.psum_tensor("p0_tr", [128, 2048], BF16))

            for st8 in range(8):
                xnf = lambda c, lo, hi, st8=st8: ar2h[
                    :, o_xnf + (st8 % 2) * NC * 512 + c * 512 + lo:
                    o_xnf + (st8 % 2) * NC * 512 + c * 512 + hi]
                xnfkey = f"xnf{st8 % 2}"
                for sub in range(4):
                    t = st8 * 4 + sub
                    slot = t % 2
                    kb.dma("Y", f"dx{slot}", lambda e, t=t, slot=slot: e.dma_start(
                        out=self.xt(slot)[:], in_=self.x0_h[t]), writes=(f"xt{slot}",))
                    self.emit_ln(self.xt(slot)[:], f"xt{slot}", self.xn_f[:], "xn_f")
                    kb.op("V", lambda e: e.tensor_copy(xn_b[:], self.xn_f[:]),
                          reads=("xn_f",), writes=("xn_b",))
                    stc = self.stats
                    kb.op("V", lambda e: e.reduce_max(stc[:, 9:10], self.xn_f[:], axis=AX.X,
                                                      apply_absolute_value=True),
                          reads=("xn_f",), writes=("st_am",))
                    kb.op("V", lambda e: e.tensor_scalar_max(stc[:, 10:11], stc[:, 9:10], 1e-5),
                          reads=("st_am",), writes=("st_amc",))
                    kb.op("V", lambda e: e.reciprocal(stc[:, 11:12], stc[:, 10:11]),
                          reads=("st_amc",), writes=("st_ra",))
                    kb.op("V", lambda e: e.tensor_single_scalar(stc[:, 12:13], stc[:, 11:12],
                                                                127.0, ALU.mult),
                          reads=("st_ra",), writes=("st_rr",))
                    kb.op("V", lambda e, t=t: e.tensor_single_scalar(
                        self.dqa[:, t:t + 1], stc[:, 10:11], 1.0 / 127.0, ALU.mult),
                        reads=("st_amc",), writes=("dqa",))
                    self.emit_round(self.xn_f[:], stc[:, 12:13], xi_b,
                                    src_keys=("xn_f", "st_rr"), dst_key="xi_b")
                    self.emit_transpose(
                        xn_b, ("xn_b",),
                        lambda cb, sub=sub, st8=st8: xnf(cb, sub * 128, (sub + 1) * 128),
                        xnfkey, NC)
                    self.emit_transpose(xi_b, ("xi_b",),
                                        lambda cb: xist[:, cb * 128:(cb + 1) * 128],
                                        "xist", NC)
                    kb.dma("Y", "dxist", lambda e, t=t: e.dma_start(
                        out=self.xi_hbm[t], in_=xist[:]), reads=("xist",), writes=("xihbm",))
                    pv = p_v0 if (t % 2 == 0) else p_v1
                    pvkey = f"p0v{t % 2}"
                    for c in range(NC):
                        lhs = xnf(c, sub * 128, (sub + 1) * 128)
                        for nh in range(2):
                            kb.op("T", lambda e, c=c, nh=nh, lhs=lhs, pv=pv: e.matmul(
                                pv[:, nh * 512:(nh + 1) * 512], lhs,
                                wqkv(c, 2048 + nh * 512, 2048 + (nh + 1) * 512),
                                start=(c == 0), stop=(c == NC - 1), skip_group_check=True),
                                reads=(xnfkey, f"wqkv{c}"), writes=(pvkey,))
                    va = ar2h[:, self.o_vaug + t * (H0 * 65):
                              self.o_vaug + (t + 1) * (H0 * 65)].rearrange(
                                  "p (h c) -> p h c", h=H0)
                    kb.op("V", lambda e, va=va, pv=pv: e.tensor_copy(
                        va[:, :, 1:65], pv[:].rearrange("p (h d) -> p h d", h=H0)),
                        reads=(pvkey,), writes=("vaug",))
                    kb.op("V", lambda e, va=va: e.memset(va[:, :, 0:1], 1.0),
                          writes=("vaug",))
                    if "v0dbg" in self.debug_taps:
                        kb.dma("Y", "dsp", lambda e, t=t: e.dma_start(
                            out=self.vdbg_hbm[t],
                            in_=self.ar2h[:, self.o_vaug + t * (H0 * 65):
                                          self.o_vaug + (t + 1) * (H0 * 65)]),
                            reads=("vaug",), writes=("vdbg",))
                for o in range(16):
                    pq = p_qk[:, (o % 2) * 512:(o % 2) * 512 + 512]
                    pqkey = f"p0qk{o % 2}"
                    for c in range(NC):
                        lhsw = wqkv(c, o * 128, (o + 1) * 128)
                        rhsx = xnf(c, 0, 512)
                        kb.op("T", lambda e, lhsw=lhsw, rhsx=rhsx, pq=pq, c=c: e.matmul(
                            pq, lhsw, rhsx,
                            start=(c == 0), stop=(c == NC - 1), skip_group_check=True),
                            reads=(xnfkey, f"wqkv{c}"), writes=(pqkey,))
                    qs = qkst(o % 2)
                    kb.op("V", lambda e, pq=pq, qs=qs: e.tensor_copy(qs, pq),
                          reads=(pqkey,), writes=(f"qkst{o % 2}",))
                    hbm = self.q0_hbm if o < 8 else self.k0_hbm
                    for b in range(4):
                        kb.dma("Y", f"dqk{o % 2}", lambda e, hbm=hbm, c=o % 8, st8=st8,
                               qs=qs, b=b: e.dma_start(
                            out=hbm[c][st8 * 4 + b],
                            in_=qs[:, b * 128:(b + 1) * 128]),
                            reads=(f"qkst{o % 2}",), writes=("qkhbm",))
            kb.flush()
        ar2.pop()

    def phaseA_l0(self):
        kb, ar2, ar4, nc = self.kb, self.ar2, self.ar4, self.nc
        kb.start_phase()
        ar2.push()
        ar4.push()
        ar2h, ar4h = self.ar2h, self.ar4h
        o_kf = ar2.take(NC * S)
        o_qb = ar2.take(2 * 1024)
        o_pab = ar2.take(2 * 2048)
        o_sta = ar2.take(1024)
        o_stb = ar2.take(1024)
        o_tst = ar2.take(1024)
        kf = lambda c: ar2h[:, o_kf + c * S: o_kf + (c + 1) * S]
        qb = lambda p: ar2h[:, o_qb + p * 1024: o_qb + (p + 1) * 1024]
        pab = lambda par: ar2h[:, o_pab + par * 2048: o_pab + (par + 1) * 2048]
        stA = ar2h[:, o_sta: o_sta + 1024]
        stB = ar2h[:, o_stb: o_stb + 1024]
        self.tstage = ar2h[:, o_tst: o_tst + 1024]
        vaug = lambda t, h: ar2h[:, self.o_vaug + t * (H0 * 65) + h * 65:
                                 self.o_vaug + t * (H0 * 65) + (h + 1) * 65]
        o_wb = ar4.take(4096)
        wb = lambda p: ar4h[:, o_wb: o_wb + 4096]

        for c in range(NC):
            for b in range(NT):
                kb.dma("Y", "dkf", lambda e, c=c, b=b: e.dma_start(
                    out=kf(c)[:, b * 128:(b + 1) * 128], in_=self.k0_hbm[c][b]),
                    reads=("qkhbm",), writes=(f"kf{c}",))

        with ExitStack() as ps:
            psS = ps.enter_context(nc.psum_tensor("pa_s", [128, 2048], F32))
            psO_A = ps.enter_context(nc.psum_tensor("pa_oa", [128, 1024], F32))
            psO_B = ps.enter_context(nc.psum_tensor("pa_ob", [128, 1024], F32))

            def emit_pv(p, j):
                pj = pab(j % 2)
                for nh in range(2):
                    kb.op("T", lambda e, p=p, j=j, nh=nh, pj=pj: e.matmul(
                        psO_A[0:65, nh * 512:(nh + 1) * 512],
                        vaug(j, 2 * p), pj[:, nh * 512:(nh + 1) * 512],
                        start=(j == 0), stop=(j == NT - 1), skip_group_check=True),
                        reads=("vaug", f"pAB{j % 2}"), writes=("psOA",))
                for nh in range(2):
                    kb.op("T", lambda e, p=p, j=j, nh=nh, pj=pj: e.matmul(
                        psO_B[0:65, nh * 512:(nh + 1) * 512],
                        vaug(j, 2 * p + 1), pj[:, 1024 + nh * 512: 1024 + (nh + 1) * 512],
                        start=(j == 0), stop=(j == NT - 1), skip_group_check=True),
                        reads=("vaug", f"pAB{j % 2}"), writes=("psOB",))

            tern = self.gen_ternarize_l0(wb, psS)
            for p in range(8):
                for qc in range(4):
                    par = (p * 4 + qc) % 2
                    for b in range(8):
                        kb.dma("Y", f"dqb{par}", lambda e, p=p, qc=qc, par=par, b=b: e.dma_start(
                            out=qb(par)[:, b * 128:(b + 1) * 128],
                            in_=self.q0_hbm[p][qc * 8 + b]),
                            reads=("qkhbm",), writes=(f"qb{par}",))
                    for j in range(NT):
                        for nh in range(2):
                            kb.op("T", lambda e, p=p, j=j, nh=nh, par=par: e.matmul(
                                psS[:, nh * 512:(nh + 1) * 512],
                                kf(p)[0:64, j * 128:(j + 1) * 128],
                                qb(par)[0:64, nh * 512:(nh + 1) * 512],
                                start=True, stop=True, skip_group_check=True),
                                reads=(f"kf{p}", f"qb{par}"), writes=("psSa",))
                        kb.op("S", lambda e, j=j: e.activation(
                            pab(j % 2)[:, 0:1024], psS[:, 0:1024], AF.Exp,
                            scale=self.consts[:, 0:1], bias=self.consts[:, 1:2]),
                            reads=("psSa", "econ"), writes=(f"pAB{j % 2}",))
                        for nh in range(2):
                            kb.op("T", lambda e, p=p, j=j, nh=nh, par=par: e.matmul(
                                psS[:, 1024 + nh * 512: 1024 + (nh + 1) * 512],
                                kf(p)[64:128, j * 128:(j + 1) * 128],
                                qb(par)[64:128, nh * 512:(nh + 1) * 512],
                                start=True, stop=True, skip_group_check=True,
                                tile_position=(64, 0)),
                                reads=(f"kf{p}", f"qb{par}"), writes=("psSb",))
                        kb.op("S", lambda e, j=j: e.activation(
                            pab(j % 2)[:, 1024:2048], psS[:, 1024:2048], AF.Exp,
                            scale=self.consts[:, 0:1], bias=self.consts[:, 1:2]),
                            reads=("psSb", "econ"), writes=(f"pAB{j % 2}",))
                        if j > 0:
                            emit_pv(p, j - 1)
                    emit_pv(p, NT - 1)
                    kb.op("V", lambda e: e.reciprocal(self.rl[0:1, 0:1024], psO_A[0:1, :]),
                          reads=("psOA",), writes=("rlX",))
                    kb.op("V", lambda e: e.tensor_copy(stA[0:65, :], psO_A[0:65, :]),
                          reads=("psOA",), writes=("stA",))
                    for nh in range(2):
                        kb.op("T", lambda e, nh=nh: e.matmul(
                            psO_A[0:65, nh * 512:(nh + 1) * 512],
                            self.onesf_row[0:1, 0:65],
                            self.rl[0:1, nh * 512:(nh + 1) * 512],
                            start=True, stop=True, skip_group_check=True),
                            reads=("rlX", "onesr"), writes=("psOA",))
                    kb.op("V", lambda e: e.tensor_tensor(stA[0:65, :], stA[0:65, :],
                                                         psO_A[0:65, :], ALU.mult),
                          reads=("stA", "psOA"), writes=("stA",))
                    kb.op("V", lambda e: e.reciprocal(self.rl[0:1, 0:1024], psO_B[0:1, :]),
                          reads=("psOB", "rlX"), writes=("rlX",))
                    kb.op("V", lambda e: e.tensor_copy(stB[0:65, :], psO_B[0:65, :]),
                          reads=("psOB",), writes=("stB",))
                    for nh in range(2):
                        kb.op("T", lambda e, nh=nh: e.matmul(
                            psO_B[0:65, nh * 512:(nh + 1) * 512],
                            self.onesf_row[0:1, 0:65],
                            self.rl[0:1, nh * 512:(nh + 1) * 512],
                            start=True, stop=True, skip_group_check=True),
                            reads=("rlX", "onesr"), writes=("psOB",))
                    kb.op("V", lambda e: e.tensor_tensor(stB[0:65, :], stB[0:65, :],
                                                         psO_B[0:65, :], ALU.mult),
                          reads=("stB", "psOB"), writes=("stB",))
                    for b in range(8):
                        kb.dma("Y", "dstA", lambda e, p=p, qc=qc, b=b: e.dma_start(
                            out=self.attn0_hbm[p][qc * 8 + b, 0:64, :],
                            in_=stA[1:65, b * 128:(b + 1) * 128]),
                            reads=("stA",), writes=("attn0hbm",))
                    for b in range(8):
                        kb.dma("Y", "dstB", lambda e, p=p, qc=qc, b=b: e.dma_start(
                            out=self.attn0_hbm[p][qc * 8 + b, 64:128, :],
                            in_=stB[1:65, b * 128:(b + 1) * 128]),
                            reads=("stB",), writes=("attn0hbm",))
                    next(tern, None)
            for _ in tern:
                pass
            kb.flush()
        ar2.pop()
        ar4.pop()

    def gen_ternarize_l0(self, wb, p_s):
        kb = self.kb
        stc = self.stats
        for wi, (w_h, t_hbm, nch, wcols, slot) in enumerate([
                (self.wgT_h, self.tg_hbm, NC, INNER, 0),
                (self.wuT_h, self.tu_hbm, NC, INNER, 1),
                (self.wdT_h, self.td_hbm, INNER // 128, D, 2)]):
            nw = float(nch * 128 * wcols)
            for c in range(nch):
                kb.dma("G", "dwb", lambda e, c=c, w_h=w_h, wcols=wcols: e.dma_start(
                    out=wb(0)[:, 0:wcols], in_=w_h[c * 128:(c + 1) * 128, :]),
                    writes=("wb0",))
                kb.op("V", lambda e, wcols=wcols: e.reduce_sum(
                    stc[:, 17:18], wb(0)[:, 0:wcols], axis=AX.X,
                    apply_absolute_value=True),
                    reads=("wb0",), writes=("st_ws1",))
                if c == 0:
                    kb.op("V", lambda e: e.tensor_copy(stc[:, 16:17], stc[:, 17:18]),
                          reads=("st_ws1",), writes=("st_ws",))
                else:
                    kb.op("V", lambda e: e.tensor_tensor(stc[:, 16:17], stc[:, 16:17],
                                                         stc[:, 17:18], ALU.add),
                          reads=("st_ws1", "st_ws"), writes=("st_ws",))
                yield
            kb.op("T", lambda e, slot=slot: e.matmul(
                p_s[0:1, slot:slot + 1], stc[:, 16:17], self.onesf_col[:],
                start=True, stop=True, skip_group_check=True),
                reads=("st_ws", "onesc"), writes=("psSa",))
            pos = 1024 + slot
            kb.op("V", lambda e, slot=slot, nw=nw, pos=pos: e.tensor_scalar(
                self.rl[0:1, pos:pos + 1], p_s[0:1, slot:slot + 1],
                1.0 / nw, 1e-5, ALU.mult, ALU.max),
                reads=("psSa",), writes=(f"wmc{slot}",))
            self.bcast_scalar(self.rl[0:1, pos:pos + 1], (f"wmc{slot}",),
                              slot, f"wmean{slot}")
            kb.op("V", lambda e, slot=slot: e.reciprocal(
                self.bcsb[:, 8 + slot:9 + slot], self.bcsb[:, slot:slot + 1]),
                reads=(f"wmean{slot}",), writes=(f"wrec{slot}",))
            yield
            for c in range(nch):
                kb.dma("G", "dwb", lambda e, c=c, w_h=w_h, wcols=wcols: e.dma_start(
                    out=wb(0)[:, 0:wcols], in_=w_h[c * 128:(c + 1) * 128, :]),
                    writes=("wb0",))
                for seg in range(wcols // 1024):
                    sl = slice(seg * 1024, (seg + 1) * 1024)
                    kb.op("V", lambda e, sl=sl, slot=slot: e.tensor_scalar(
                        self.ga_f[:], wb(0)[:, sl], self.bcsb[:, 8 + slot:9 + slot],
                        0.5, ALU.mult, ALU.is_gt),
                        reads=("wb0", f"wrec{slot}"), writes=("rn_a",))
                    kb.op("V", lambda e, sl=sl, slot=slot: e.tensor_scalar(
                        self.gb_f[:], wb(0)[:, sl], self.bcsb[:, 8 + slot:9 + slot],
                        -0.5, ALU.mult, ALU.is_lt),
                        reads=("wb0", f"wrec{slot}"), writes=("rn_b",))
                    kb.op("V", lambda e: e.tensor_tensor(
                        self.tstage[:], self.ga_f[:], self.gb_f[:], ALU.subtract),
                        reads=("rn_a", "rn_b"), writes=("tstage",))
                    kb.dma("Y", "dtst", lambda e, c=c, seg=seg, t_hbm=t_hbm,
                           blk=(wcols == INNER): e.dma_start(
                        out=(t_hbm[c][seg] if blk else t_hbm[c]), in_=self.tstage[:]),
                        reads=("tstage",), writes=("ternhbm",))
                yield

    def phaseC_l0(self):
        kb, ar2, ar4, nc = self.kb, self.ar2, self.ar4, self.nc
        kb.start_phase()
        ar2.push()
        ar4.push()
        ar2h, ar4h = self.ar2h, self.ar4h
        o_wg = ar2.take(NC * INNER)
        o_wu = ar2.take(NC * INNER)
        o_xit = ar2.take(2 * 1024)
        o_x2b = ar2.take(2048)
        o_x2st = ar2.take(1024)
        o_sil = ar2.take(1024)
        o_sca = ar2.take(2 * 1024)
        wg = lambda c: ar2h[:, o_wg + c * INNER: o_wg + (c + 1) * INNER]
        wu = lambda c: ar2h[:, o_wu + c * INNER: o_wu + (c + 1) * INNER]
        xit = lambda p: ar2h[:, o_xit + p * 1024: o_xit + (p + 1) * 1024]
        xi2_b = lambda s: ar2h[:, o_x2b + s * 1024: o_x2b + (s + 1) * 1024]
        xi2st = ar2h[:, o_x2st: o_x2st + 1024]
        sil_b = ar2h[:, o_sil: o_sil + 1024]
        sca = ar2h[:, o_sca: o_sca + 1024]
        scb = ar2h[:, o_sca + 1024: o_sca + 2048]
        o_g2 = ar4.take(4096)
        g2 = ar4h[:, o_g2: o_g2 + 4096]

        for c in range(NC):
            for sg in range(4):
                kb.dma("G", "dwg", lambda e, c=c, sg=sg: e.dma_start(
                    out=wg(c)[:, sg * 1024:(sg + 1) * 1024], in_=self.tg_hbm[c][sg]),
                    reads=("ternhbm",), writes=(f"twg{c}",))
            for sg in range(4):
                kb.dma("G", "dwg", lambda e, c=c, sg=sg: e.dma_start(
                    out=wu(c)[:, sg * 1024:(sg + 1) * 1024], in_=self.tu_hbm[c][sg]),
                    reads=("ternhbm",), writes=(f"twu{c}",))

        with ExitStack() as ps:
            p_g = [ps.enter_context(nc.psum_tensor("pc_g0", [128, 1024], F32)),
                   ps.enter_context(nc.psum_tensor("pc_g1", [128, 1024], F32))]
            p_u0 = ps.enter_context(nc.psum_tensor("pc_u0", [128, 1024], F32))
            p_u = [p_u0, p_u0]
            self.p_tr = ps.enter_context(nc.psum_tensor("pc_tr", [128, 2048], BF16))
            stc = self.stats

            def emit_seg_round(t, seg):
                sl = slice(seg * 1024, (seg + 1) * 1024)
                self.emit_round(g2[:, sl], stc[:, 12:13], xi2_b(seg % 2),
                                src_keys=(f"g2i{seg}", "st_rr"), dst_key=f"x2b{seg % 2}",
                                sc=(sca[:], scb[:], "rc_a", "rc_b"))

            def emit_seg_spill(t, seg):
                self.emit_transpose(xi2_b(seg % 2), (f"x2b{seg % 2}",),
                                    lambda cb: xi2st[:, cb * 128:(cb + 1) * 128],
                                    "xi2st", 8, evac_eng="S")
                kb.dma("Y", "dx2st", lambda e, t=t, seg=seg: e.dma_start(
                    out=self.xi2_hbm[t][seg], in_=xi2st[:]),
                    reads=("xi2st",), writes=("xi2hbm",))

            def emit_combine(t):
                kb.op("V", lambda e: e.tensor_tensor(stc[:, 17:18], stc[:, 13:14],
                                                     stc[:, 14:15], ALU.max),
                      reads=("pm0", "pm1"), writes=("cmb0",))
                kb.op("V", lambda e: e.tensor_tensor(stc[:, 18:19], stc[:, 15:16],
                                                     stc[:, 16:17], ALU.max),
                      reads=("pm2", "pm3"), writes=("cmb1",))
                kb.op("V", lambda e: e.tensor_tensor(stc[:, 9:10], stc[:, 17:18],
                                                     stc[:, 18:19], ALU.max),
                      reads=("cmb0", "cmb1"), writes=("st_am",))
                kb.op("V", lambda e: e.tensor_scalar_max(stc[:, 10:11], stc[:, 9:10], 1e-5),
                      reads=("st_am",), writes=("st_amc",))
                kb.op("V", lambda e: e.reciprocal(stc[:, 11:12], stc[:, 10:11]),
                      reads=("st_amc",), writes=("st_ra",))
                kb.op("V", lambda e: e.tensor_single_scalar(stc[:, 12:13], stc[:, 11:12],
                                                            127.0, ALU.mult),
                      reads=("st_ra",), writes=("st_rr",))
                kb.op("V", lambda e, t=t: e.tensor_single_scalar(
                    self.dq2a[:, t:t + 1], stc[:, 10:11], 1.0 / 127.0, ALU.mult),
                    reads=("st_amc",), writes=("dq2a",))

            for t in range(NT):
                par = t % 2
                kb.dma("Y", f"dxi{par}", lambda e, t=t, par=par: e.dma_start(
                    out=xit(par), in_=self.xi_hbm[t]),
                    reads=("xihbm",), writes=(f"xit{par}",))
                for icp in range(4):
                    pp = icp % 2
                    pg, pu = p_g[pp], p_u[0]
                    for c in range(NC):
                        lhs = xit(par)[:, c * 128:(c + 1) * 128]
                        for nh in range(2):
                            kb.op("T", lambda e, c=c, nh=nh, icp=icp, lhs=lhs, pg=pg: e.matmul(
                                pg[:, nh * 512:(nh + 1) * 512], lhs,
                                wg(c)[:, icp * 1024 + nh * 512: icp * 1024 + (nh + 1) * 512],
                                start=(c == 0), stop=(c == NC - 1), skip_group_check=True),
                                reads=(f"xit{par}", f"twg{c}"), writes=(f"pcg{pp}",))
                            kb.op("T", lambda e, c=c, nh=nh, icp=icp, lhs=lhs, pu=pu: e.matmul(
                                pu[:, nh * 512:(nh + 1) * 512], lhs,
                                wu(c)[:, icp * 1024 + nh * 512: icp * 1024 + (nh + 1) * 512],
                                start=(c == 0), stop=(c == NC - 1), skip_group_check=True),
                                reads=(f"xit{par}", f"twu{c}"), writes=("pcu",))
                    if t > 0:
                        emit_seg_round(t - 1, icp)
                    kb.op("V", lambda e, t=t: e.tensor_scalar(
                        self.stats[:, 20:21], self.dqa[:, t:t + 1],
                        self.bcsb[:, 0:1], None, ALU.mult),
                        reads=("dqa", "wmean0"), writes=("st_sg",))
                    kb.op("V", lambda e, t=t: e.tensor_scalar(
                        self.stats[:, 21:22], self.dqa[:, t:t + 1],
                        self.bcsb[:, 1:2], None, ALU.mult),
                        reads=("dqa", "wmean1"), writes=("st_su",))
                    kb.op("S", lambda e, pg=pg, pp=pp: e.activation(
                        sil_b[:], pg[:], AF.Silu, scale=self.stats[:, 20:21]),
                        reads=(f"pcg{pp}", "st_sg"), writes=("silb",))
                    kb.op("V", lambda e, pu=pu, icp=icp: e.scalar_tensor_tensor(
                        g2[:, icp * 1024:(icp + 1) * 1024], pu[:], self.stats[:, 21:22],
                        sil_b[:], ALU.mult, ALU.mult),
                        reads=("pcu", "st_su", "silb"), writes=(f"g2i{icp}",))
                    kb.op("V", lambda e, icp=icp: e.reduce_max(
                        stc[:, 13 + icp:14 + icp], g2[:, icp * 1024:(icp + 1) * 1024],
                        axis=AX.X, apply_absolute_value=True),
                        reads=(f"g2i{icp}",), writes=(f"pm{icp}",))
                    if t > 0 and icp >= 1:
                        emit_seg_spill(t - 1, icp - 1)
                if t > 0:
                    emit_seg_spill(t - 1, 3)
                emit_combine(t)
            for seg in range(4):
                emit_seg_round(NT - 1, seg)
                emit_seg_spill(NT - 1, seg)
            kb.flush()
        ar2.pop()
        ar4.pop()

    def phaseD_l0(self):
        kb, ar2, nc = self.kb, self.ar2, self.nc
        ar2.push()
        ar2h = self.ar2h
        o_wd = ar2.take(32 * D)
        o_wo = ar2.take(NC * D)
        o_x2t = ar2.take(2 * 4096)
        o_att = ar2.take(2 * 1024)
        wd = lambda c: ar2h[:, o_wd + c * D: o_wd + (c + 1) * D]
        wo = lambda c: ar2h[:, o_wo + c * D: o_wo + (c + 1) * D]
        x2t = lambda p: ar2h[:, o_x2t + p * 4096: o_x2t + (p + 1) * 4096]
        att = lambda p: ar2h[:, o_att + p * 1024: o_att + (p + 1) * 1024]
        kb.start_phase()

        for c in range(32):
            kb.dma("G", "dwd", lambda e, c=c: e.dma_start(out=wd(c), in_=self.td_hbm[c]),
                   reads=("ternhbm",), writes=(f"twd{c}",))
        for c in range(NC):
            kb.dma("G", "dwd", lambda e, c=c: e.dma_start(
                out=wo(c), in_=self.woT_h[c * 128:(c + 1) * 128, :]),
                writes=(f"wo{c}",))

        with ExitStack() as ps:
            p_d = [ps.enter_context(nc.psum_tensor(f"pd_d{i}", [128, 1024], F32))
                   for i in range(2)]
            p_o = [ps.enter_context(nc.psum_tensor(f"pd_o{i}", [128, 1024], F32))
                   for i in range(2)]

            for t in range(NT):
                par = t % 2
                for sg in range(4):
                    kb.dma("Y", f"dxi{par}", lambda e, t=t, par=par, sg=sg: e.dma_start(
                        out=x2t(par)[:, sg * 1024:(sg + 1) * 1024],
                        in_=self.xi2_hbm[t][sg]),
                        reads=("xi2hbm",), writes=(f"x2t{par}",))
                for c_ in range(NC):
                    kb.dma("Y", f"dat{par}", lambda e, t=t, par=par, c_=c_: e.dma_start(
                        out=att(par)[:, c_ * 128:(c_ + 1) * 128],
                        in_=self.attn0_hbm[c_, t]),
                        reads=("attn0hbm",), writes=(f"att{par}",))
                kb.dma("Y", f"dx{par}", lambda e, t=t, par=par: e.dma_start(
                    out=self.xt(par)[:], in_=self.x0_h[t]), writes=(f"xt{par}",))
                pd, po = p_d[par], p_o[par]
                for c in range(32):
                    lhs = x2t(par)[:, c * 128:(c + 1) * 128]
                    for nh in range(2):
                        kb.op("T", lambda e, c=c, nh=nh, lhs=lhs, pd=pd: e.matmul(
                            pd[:, nh * 512:(nh + 1) * 512], lhs,
                            wd(c)[:, nh * 512:(nh + 1) * 512],
                            start=(c == 0), stop=(c == 31), skip_group_check=True),
                            reads=(f"x2t{par}", f"twd{c}"), writes=(f"pdd{par}",))
                for c in range(NC):
                    lhs = att(par)[:, c * 128:(c + 1) * 128]
                    for nh in range(2):
                        kb.op("T", lambda e, c=c, nh=nh, lhs=lhs, po=po: e.matmul(
                            po[:, nh * 512:(nh + 1) * 512], lhs,
                            wo(c)[:, nh * 512:(nh + 1) * 512],
                            start=(c == 0), stop=(c == NC - 1), skip_group_check=True),
                            reads=(f"att{par}", f"wo{c}"), writes=(f"pdo{par}",))
                kb.op("V", lambda e, t=t: e.tensor_scalar(
                    self.stats[:, 22:23], self.dq2a[:, t:t + 1],
                    self.bcsb[:, 2:3], None, ALU.mult),
                    reads=("dq2a", "wmean2"), writes=("st_sd",))
                if self.dbg_raw == "pd":
                    kb.op("S", lambda e, pd=pd: e.activation(
                        self.ga_f[:], pd[:], AF.Silu),
                        reads=(f"pdd{par}",), writes=("rn_a",))
                else:
                    kb.op("S", lambda e, pd=pd: e.activation(
                        self.ga_f[:], pd[:], AF.Silu, scale=self.stats[:, 22:23]),
                        reads=(f"pdd{par}", "st_sd"), writes=("rn_a",))
                if "g0dbg" in self.debug_taps:
                    kb.dma("Y", "dsp", lambda e, t=t: e.dma_start(
                        out=self.g0dbg_hbm[t], in_=self.ga_f[:]),
                        reads=("rn_a",), writes=("g0dbg",))
                    kb.op("V", lambda e, po=po: e.tensor_copy(self.sq_f[:], po[:]),
                          reads=(f"pdo{par}",), writes=("rn_d",))
                    kb.dma("Y", "dsp", lambda e, t=t: e.dma_start(
                        out=self.s0dbg_hbm[t], in_=self.sq_f[:]),
                        reads=("rn_d",), writes=("s0dbg",))
                kb.op("V", lambda e, po=po: e.tensor_tensor(
                    self.gb_f[:], po[:], self.ga_f[:], ALU.mult),
                    reads=(f"pdo{par}", "rn_a"), writes=("rn_b",))
                kb.op("V", lambda e, par=par: e.tensor_tensor(
                    self.xn_f[:], self.xt(par)[:], self.gb_f[:], ALU.add),
                    reads=(f"xt{par}", "rn_b"), writes=("xn_f",))
                kb.dma("Y", "dx1s", lambda e, t=t: e.dma_start(
                    out=self.x1_hbm[t], in_=self.xn_f[:]),
                    reads=("xn_f",), writes=("x1hbm",))
            kb.flush()
        ar2.pop()

    def phase0_l1(self):
        kb, ar2, ar4, nc = self.kb, self.ar2, self.ar4, self.nc
        ar2.push()
        ar4.push()
        ar2h, ar4h = self.ar2h, self.ar4h
        o_wq = ar2.take(NC * D)
        o_wk = ar2.take(NC * 512)
        o_wv = ar2.take(NC * 512)
        self.o_k1 = ar2.take(HKV1 * S)
        self.o_v1 = ar2.take(NT * HKV1 * 129)
        o_xnb = ar2.take(2 * 1024)
        o_qrb = ar2.take(2 * 1024)
        o_krb = ar2.take(2 * 512)
        o_xib = ar2.take(2 * 1024)
        o_sxn = ar2.take(2 * 1024)
        o_stq = ar2.take(2 * 1024)
        o_sti = ar2.take(2 * 1024)
        wq1 = lambda c: ar2h[:, o_wq + c * D: o_wq + (c + 1) * D]
        wk1 = lambda c: ar2h[:, o_wk + c * 512: o_wk + (c + 1) * 512]
        wv1 = lambda c: ar2h[:, o_wv + c * 512: o_wv + (c + 1) * 512]
        xn_b = lambda p: ar2h[:, o_xnb + p * 1024: o_xnb + (p + 1) * 1024]
        qr_b = lambda p: ar2h[:, o_qrb + p * 1024: o_qrb + (p + 1) * 1024]
        kr_b = lambda p: ar2h[:, o_krb + p * 512: o_krb + (p + 1) * 512]
        xi_b = lambda p: ar2h[:, o_xib + p * 1024: o_xib + (p + 1) * 1024]
        sxn = lambda p: ar2h[:, o_sxn + p * 1024: o_sxn + (p + 1) * 1024]
        stq = lambda p: ar2h[:, o_stq + p * 1024: o_stq + (p + 1) * 1024]
        sti = lambda p: ar2h[:, o_sti + p * 1024: o_sti + (p + 1) * 1024]
        o_tab = ar4.take(2 * 256)
        tab = lambda p: ar4h[:, o_tab + p * 256: o_tab + (p + 1) * 256]
        o_qkf = ar4.take(1536)
        qkf = ar4h[:, o_qkf: o_qkf + 1536]
        o_rp = ar4.take(2048)
        rp_a = ar4h[:, o_rp: o_rp + 1024]
        rp_b = ar4h[:, o_rp + 1024: o_rp + 2048]
        kb.start_phase()

        for c in range(NC):
            kb.dma("G", "dw1a", lambda e, c=c: e.dma_start(
                out=wq1(c), in_=self.wq1_h[c * 128:(c + 1) * 128, :]), writes=(f"w1q{c}",))
            kb.dma("G", "dw1a", lambda e, c=c: e.dma_start(
                out=wk1(c), in_=self.wk1_h[c * 128:(c + 1) * 128, :]), writes=(f"w1k{c}",))
            kb.dma("G", "dw1a", lambda e, c=c: e.dma_start(
                out=wv1(c), in_=self.wv1_h[c * 128:(c + 1) * 128, :]), writes=(f"w1v{c}",))

        kb.dma("Y", "dasc", lambda e: e.dma_start(out=self.rl[0:1, 1028:1029], in_=self.ascale_h),
               writes=("asc_raw",))
        kb.op("V", lambda e: e.tensor_scalar_max(self.rl[0:1, 1029:1030],
                                                 self.rl[0:1, 1028:1029], 1e-5),
              reads=("asc_raw",), writes=("asc_s",))
        self.bcast_scalar(self.rl[0:1, 1029:1030], ("asc_s",), 3, "s_bc")
        kb.op("V", lambda e: e.reciprocal(self.bcsb[:, 8 + 3:8 + 4], self.bcsb[:, 3:4]),
              reads=("s_bc",), writes=("rs_bc",))
        kb.op("V", lambda e: e.tensor_single_scalar(
            self.b1bf[0:1, :], self.b1row[0:1, :], self.bcsb[0:1, 8 + 3:8 + 4], ALU.mult),
            reads=("b1row", "rs_bc"), writes=("b1bf",))

        with ExitStack() as ps:
            p_q = [ps.enter_context(nc.psum_tensor(f"p1_q{i}", [128, 1024], F32))
                   for i in range(2)]
            p_kv0 = ps.enter_context(nc.psum_tensor("p1_kv0", [128, 1024], F32))
            p_kv = [p_kv0, p_kv0]
            self.p_tr = ps.enter_context(nc.psum_tensor("p1_tr", [128, 2048], BF16))

            def emit_pre(t):
                par = t % 2
                pq, pkv = p_q[par], p_kv[par]
                kb.op("V", lambda e, pq=pq: e.tensor_copy(qkf[:, 0:1024], pq[:]),
                      reads=(f"p1q{par}",), writes=("qkvf",))
                kb.op("V", lambda e, pkv=pkv: e.tensor_copy(qkf[:, 1024:1536], pkv[:, 0:512]),
                      reads=("p1kv",), writes=("qkvf",))
                va = ar2h[:, self.o_v1 + t * (HKV1 * 129):
                          self.o_v1 + (t + 1) * (HKV1 * 129)].rearrange(
                              "p (h c) -> p h c", h=HKV1)
                kb.op("V", lambda e, va=va, pkv=pkv: e.tensor_copy(
                    va[:, :, 1:129], pkv[:, 512:1024].rearrange("p (h d) -> p h d", h=HKV1)),
                    reads=("p1kv",), writes=("v1aug",))
                kb.op("V", lambda e, va=va: e.memset(va[:, :, 0:1], 1.0),
                      writes=("v1aug",))

            def emit_front(t):
                par = t % 2
                kb.dma("Y", f"dx{par}", lambda e, t=t, par=par: e.dma_start(
                    out=self.xt(par)[:], in_=self.x1_hbm[t]),
                    reads=("x1hbm",), writes=(f"xt{par}",))
                kb.dma("Y", f"dtb{par}", lambda e, t=t, par=par: e.dma_start(
                    out=tab(par)[:, 0:128], in_=self.cosq_h[t * 128:(t + 1) * 128, :]),
                    writes=(f"tab{par}",))
                kb.dma("Y", f"dtb{par}", lambda e, t=t, par=par: e.dma_start(
                    out=tab(par)[:, 128:256], in_=self.sinq_h[t * 128:(t + 1) * 128, :]),
                    writes=(f"tab{par}b",))
                self.emit_ln(self.xt(par)[:], f"xt{par}", self.xn_f[:], "xn_f")
                kb.op("V", lambda e, par=par: e.tensor_copy(xn_b(par)[:], self.xn_f[:]),
                      reads=("xn_f",), writes=(f"xnb{par}",))

            def emit_rope(t):
                par = t % 2
                cos8 = tab(par)[:, 0:128].unsqueeze(1).broadcast_to((128, 8, 128))
                qv = qkf[:, 0:1024].rearrange("p (h w) -> p h w", h=8)
                qv4 = qkf[:, 0:1024].rearrange("p (h x w) -> p h x w", h=8, x=2)
                gb4 = rp_b[:].rearrange("p (h x w) -> p h x w", h=8, x=2)
                sinlo = tab(par)[:, 128:192].unsqueeze(1).broadcast_to((128, 8, 64))
                sinhi = tab(par)[:, 192:256].unsqueeze(1).broadcast_to((128, 8, 64))
                kb.op("V", lambda e, qv=qv, cos8=cos8: e.tensor_tensor(
                    rp_a[:].rearrange("p (h w) -> p h w", h=8), qv, cos8, ALU.mult),
                    reads=("qkvf", f"tab{par}"), writes=("rp_a",))
                kb.op("V", lambda e, gb4=gb4, qv4=qv4, sinlo=sinlo: e.tensor_tensor(
                    gb4[:, :, 0, :], qv4[:, :, 1, :], sinlo, ALU.mult),
                    reads=("qkvf", f"tab{par}b"), writes=("rp_b",))
                kb.op("V", lambda e, gb4=gb4, qv4=qv4, sinhi=sinhi: e.tensor_tensor(
                    gb4[:, :, 1, :], qv4[:, :, 0, :], sinhi, ALU.mult),
                    reads=("qkvf", f"tab{par}b"), writes=("rp_b",))
                kb.op("V", lambda e, par=par: e.tensor_tensor(
                    qr_b(par)[:], rp_a[:], rp_b[:], ALU.add),
                    reads=("rp_a", "rp_b"), writes=(f"qrb{par}",))
                kv4 = qkf[:, 1024:1536].rearrange("p (h x w) -> p h x w", h=4, x=2)
                gk4 = rp_b[:, 0:512].rearrange("p (h x w) -> p h x w", h=4, x=2)
                sinlo4 = tab(par)[:, 128:192].unsqueeze(1).broadcast_to((128, 4, 64))
                sinhi4 = tab(par)[:, 192:256].unsqueeze(1).broadcast_to((128, 4, 64))
                kcos = tab(par)[:, 0:128].unsqueeze(1).broadcast_to((128, 4, 128))
                kvv = qkf[:, 1024:1536].rearrange("p (h w) -> p h w", h=4)
                kb.op("V", lambda e, kvv=kvv, kcos=kcos: e.tensor_tensor(
                    rp_a[:, 0:512].rearrange("p (h w) -> p h w", h=4),
                    kvv, kcos, ALU.mult),
                    reads=("qkvf", f"tab{par}"), writes=("rp_a",))
                kb.op("V", lambda e, gk4=gk4, kv4=kv4, sinlo4=sinlo4: e.tensor_tensor(
                    gk4[:, :, 0, :], kv4[:, :, 1, :], sinlo4, ALU.mult),
                    reads=("qkvf", f"tab{par}b"), writes=("rp_b",))
                kb.op("V", lambda e, gk4=gk4, kv4=kv4, sinhi4=sinhi4: e.tensor_tensor(
                    gk4[:, :, 1, :], kv4[:, :, 0, :], sinhi4, ALU.mult),
                    reads=("qkvf", f"tab{par}b"), writes=("rp_b",))
                kb.op("V", lambda e, par=par: e.tensor_tensor(
                    kr_b(par)[:], rp_a[:, 0:512], rp_b[:, 0:512], ALU.add),
                    reads=("rp_a", "rp_b"), writes=(f"krb{par}",))

            def emit_mid(t):
                par = t % 2
                self.emit_transpose(xn_b(par), (f"xnb{par}",),
                                    lambda cb, par=par: sxn(par)[:, cb * 128:(cb + 1) * 128],
                                    f"sxn{par}", NC)
                pq, pkv = p_q[par], p_kv[par]
                for c in range(NC):
                    lhs = sxn(par)[:, c * 128:(c + 1) * 128]
                    for nh in range(2):
                        kb.op("T", lambda e, c=c, nh=nh, lhs=lhs, pq=pq: e.matmul(
                            pq[:, nh * 512:(nh + 1) * 512], lhs,
                            wq1(c)[:, nh * 512:(nh + 1) * 512],
                            start=(c == 0), stop=(c == NC - 1), skip_group_check=True),
                            reads=(f"sxn{par}", f"w1q{c}"), writes=(f"p1q{par}",))
                    kb.op("T", lambda e, c=c, lhs=lhs, pkv=pkv: e.matmul(
                        pkv[:, 0:512], lhs, wk1(c),
                        start=(c == 0), stop=(c == NC - 1), skip_group_check=True),
                        reads=(f"sxn{par}", f"w1k{c}"), writes=("p1kv",))
                    kb.op("T", lambda e, c=c, lhs=lhs, pkv=pkv: e.matmul(
                        pkv[:, 512:1024], lhs, wv1(c),
                        start=(c == 0), stop=(c == NC - 1), skip_group_check=True),
                        reads=(f"sxn{par}", f"w1v{c}"), writes=("p1kv",))

            def emit_quant(t):
                par = t % 2
                self.emit_round(self.xn_f[:], self.bcsb[:, 8 + 3:8 + 4], xi_b(par),
                                src_keys=("xn_f", "rs_bc"), dst_key=f"xib{par}", clip=127.0)

            def emit_back(t):
                par = t % 2
                self.emit_transpose(qr_b(par), (f"qrb{par}",),
                                    lambda cb, par=par: stq(par)[:, cb * 128:(cb + 1) * 128],
                                    f"stq{par}", NC, evac_eng="S")
                for h in range(H1):
                    kb.dma("Y", "dqr", lambda e, h=h, t=t, par=par: e.dma_start(
                        out=self.qr1_hbm[h, t],
                        in_=stq(par)[:, h * 128:(h + 1) * 128]),
                        reads=(f"stq{par}",), writes=("qr1hbm",))
                self.emit_transpose(kr_b(par), (f"krb{par}",),
                                    lambda cb, t=t: ar2h[:, self.o_k1 + cb * S + t * 128:
                                                         self.o_k1 + cb * S + (t + 1) * 128],
                                    "k1f", 4, evac_eng="S")
                self.emit_transpose(xi_b(par), (f"xib{par}",),
                                    lambda cb, par=par: sti(par)[:, cb * 128:(cb + 1) * 128],
                                    f"sti{par}", NC, evac_eng="S")
                kb.dma("Y", "dxint", lambda e, t=t, par=par: e.dma_start(
                    out=self.xint_hbm[t], in_=sti(par)[:]),
                    reads=(f"sti{par}",), writes=("xinthbm",))

            for t in range(NT):
                if t > 0:
                    emit_pre(t - 1)
                emit_front(t)
                emit_quant(t)
                if t > 0:
                    emit_rope(t - 1)
                emit_mid(t)
                if t > 0:
                    emit_back(t - 1)
            emit_pre(NT - 1)
            emit_rope(NT - 1)
            emit_back(NT - 1)
            kb.flush()
        ar2.pop()
        ar4.pop()

    def phaseA_l1(self):
        kb, ar2, nc = self.kb, self.ar2, self.nc
        ar2.push()
        ar2h = self.ar2h
        o_qb = ar2.take(2 * 1024)
        o_p = ar2.take(2 * 1024)
        o_sta = ar2.take(1024)
        o_stb = ar2.take(1024)
        qb = lambda p: ar2h[:, o_qb + p * 1024: o_qb + (p + 1) * 1024]
        pt = lambda p: ar2h[:, o_p + p * 1024: o_p + (p + 1) * 1024]
        stA = ar2h[:, o_sta: o_sta + 1024]
        stB = ar2h[:, o_stb: o_stb + 1024]
        k1 = lambda c: ar2h[:, self.o_k1 + c * S: self.o_k1 + (c + 1) * S]
        v1 = lambda t, h: ar2h[:, self.o_v1 + t * (HKV1 * 129) + h * 129:
                               self.o_v1 + t * (HKV1 * 129) + (h + 1) * 129]
        kb.start_phase()

        with ExitStack() as ps:
            psS = [ps.enter_context(nc.psum_tensor(f"pl_s{i}", [128, 1024], F32))
                   for i in range(2)]
            psO_A = ps.enter_context(nc.psum_tensor("pl_oa", [128, 1024], F32))
            psO_B = ps.enter_context(nc.psum_tensor("pl_ob", [128, 1024], F32))

            def emit_pv1(kv, j, jmax, qc):
                o = j - 8 * qc
                w0 = max(0, o * 128)
                segs = [(w0, 512), (512, 1024)] if w0 < 512 else [(w0, 1024)]
                pp = j % 2
                for (a, b) in segs:
                    kb.op("T", lambda e, j=j, a=a, b=b, kv=kv, pp=pp, jmax=jmax: e.matmul(
                        psO_A[0:65, a:b], v1(j, kv)[:, 0:65], pt(pp)[:, a:b],
                        start=(j == 0), stop=(j == jmax), skip_group_check=True),
                        reads=("v1aug", f"pt{pp}"), writes=("psOA1",))
                    kb.op("T", lambda e, j=j, a=a, b=b, kv=kv, pp=pp, jmax=jmax: e.matmul(
                        psO_B[0:64, a:b], v1(j, kv)[:, 65:129], pt(pp)[:, a:b],
                        start=(j == 0), stop=(j == jmax), skip_group_check=True),
                        reads=("v1aug", f"pt{pp}"), writes=("psOB1",))

            for h in range(H1):
                kv = h // 2
                for qc in range(4):
                    par = (h * 4 + qc) % 2
                    for b in range(8):
                        kb.dma("Y", f"dqb{par}", lambda e, h=h, qc=qc, par=par, b=b: e.dma_start(
                            out=qb(par)[:, b * 128:(b + 1) * 128],
                            in_=self.qr1_hbm[h][qc * 8 + b]),
                            reads=("qr1hbm",), writes=(f"qb1_{par}",))
                    jmax = 8 * qc + 7
                    for j in range(jmax + 1):
                        o = j - 8 * qc
                        w0 = max(0, o * 128)
                        pp = j % 2
                        ps_ = psS[pp]
                        pkey = f"plS{pp}"
                        segs = [(w0, 512), (512, 1024)] if w0 < 512 else [(w0, 1024)]
                        for (a, b) in segs:
                            kb.op("T", lambda e, h=h, j=j, a=a, b=b, kv=kv, par=par, ps_=ps_: e.matmul(
                                ps_[:, a:b], k1(kv)[:, j * 128:(j + 1) * 128],
                                qb(par)[:, a:b],
                                start=True, stop=True, skip_group_check=True),
                                reads=("k1f", f"qb1_{par}"), writes=(pkey,))
                        kb.op("S", lambda e, w0=w0, ps_=ps_, pp=pp: e.activation(
                            pt(pp)[:, w0:1024], ps_[:, w0:1024], AF.Exp,
                            scale=self.consts[:, 2:3]),
                            reads=(pkey, "econ"), writes=(f"pt{pp}",))
                        if o >= 0:
                            kb.op("V", lambda e, w0=w0, pp=pp: e.tensor_tensor(
                                pt(pp)[:, w0:w0 + 128], pt(pp)[:, w0:w0 + 128],
                                self.triu[:], ALU.mult),
                                reads=(f"pt{pp}", "triu"), writes=(f"pt{pp}",))
                        if j > 0:
                            emit_pv1(kv, j - 1, jmax, qc)
                    emit_pv1(kv, jmax, jmax, qc)
                    kb.op("V", lambda e: e.reciprocal(self.rl[0:1, 0:1024], psO_A[0:1, :]),
                          reads=("psOA1",), writes=("rlX",))
                    kb.op("V", lambda e: e.tensor_copy(stA[0:65, :], psO_A[0:65, :]),
                          reads=("psOA1",), writes=("stA1",))
                    kb.op("V", lambda e: e.tensor_copy(stB[0:64, :], psO_B[0:64, :]),
                          reads=("psOB1",), writes=("stB1",))
                    for nh in range(2):
                        kb.op("T", lambda e, nh=nh: e.matmul(
                            psO_A[0:128, nh * 512:(nh + 1) * 512],
                            self.onesf_row[0:1, 0:128],
                            self.rl[0:1, nh * 512:(nh + 1) * 512],
                            start=True, stop=True, skip_group_check=True),
                            reads=("rlX", "onesr", "stA1"), writes=("psOA1",))
                    kb.op("V", lambda e: e.tensor_tensor(stA[0:65, :], stA[0:65, :],
                                                         psO_A[0:65, :], ALU.mult),
                          reads=("stA1", "psOA1"), writes=("stA1",))
                    kb.op("V", lambda e: e.tensor_tensor(stB[0:64, :], stB[0:64, :],
                                                         psO_A[0:64, :], ALU.mult),
                          reads=("stB1", "psOA1"), writes=("stB1",))
                    for b in range(8):
                        kb.dma("Y", "dstA", lambda e, h=h, qc=qc, b=b: e.dma_start(
                            out=self.attn1_hbm[h][qc * 8 + b, 0:64, :],
                            in_=stA[1:65, b * 128:(b + 1) * 128]),
                            reads=("stA1",), writes=("attn1hbm",))
                    for b in range(8):
                        kb.dma("Y", "dstB", lambda e, h=h, qc=qc, b=b: e.dma_start(
                            out=self.attn1_hbm[h][qc * 8 + b, 64:128, :],
                            in_=stB[0:64, b * 128:(b + 1) * 128]),
                            reads=("stB1",), writes=("attn1hbm",))
            kb.flush()
        ar2.pop()

    def phaseC_l1(self):
        kb, ar2, nc = self.kb, self.ar2, self.nc
        ar2.push()
        ar2h = self.ar2h
        o_w1 = ar2.take(NC * D)
        o_wo = ar2.take(NC * D)
        o_xit = ar2.take(2 * 1024)
        o_att = ar2.take(2 * 1024)
        w1 = lambda c: ar2h[:, o_w1 + c * D: o_w1 + (c + 1) * D]
        wo = lambda c: ar2h[:, o_wo + c * D: o_wo + (c + 1) * D]
        xit = lambda p: ar2h[:, o_xit + p * 1024: o_xit + (p + 1) * 1024]
        att = lambda p: ar2h[:, o_att + p * 1024: o_att + (p + 1) * 1024]
        kb.start_phase()

        for c in range(NC):
            kb.dma("G", "dwt", lambda e, c=c: e.dma_start(
                out=self.rnd_i[:, 0:1024], in_=self.ternT_h[c * 128:(c + 1) * 128, :]),
                reads=("rn_ci",), writes=("rn_ci",))
            kb.op("V", lambda e, c=c: e.tensor_scalar_add(w1(c), self.rnd_i[:, 0:1024], -1.0),
                  reads=("rn_ci",), writes=(f"tw1{c}",))
        for c in range(NC):
            kb.dma("G", "dwo1", lambda e, c=c: e.dma_start(
                out=wo(c), in_=self.wo1_h[c * 128:(c + 1) * 128, :]), writes=(f"wo1{c}",))

        with ExitStack() as ps:
            p_y = [ps.enter_context(nc.psum_tensor(f"py_{i}", [128, 1024], F32))
                   for i in range(2)]
            p_s = [ps.enter_context(nc.psum_tensor(f"ps1_{i}", [128, 1024], F32))
                   for i in range(2)]

            for t in range(NT):
                par = t % 2
                kb.dma("Y", f"dxi{par}", lambda e, t=t, par=par: e.dma_start(
                    out=xit(par), in_=self.xint_hbm[t]),
                    reads=("xinthbm",), writes=(f"xt1_{par}",))
                for c_ in range(NC):
                    kb.dma("Y", f"dat{par}", lambda e, t=t, par=par, c_=c_: e.dma_start(
                        out=att(par)[:, c_ * 128:(c_ + 1) * 128],
                        in_=self.attn1_hbm[c_, t]),
                        reads=("attn1hbm",), writes=(f"at1_{par}",))
                kb.dma("Y", f"dx{par}", lambda e, t=t, par=par: e.dma_start(
                    out=self.xt(par)[:], in_=self.x1_hbm[t]),
                    reads=("x1hbm",), writes=(f"xt{par}",))
                py, psn = p_y[par], p_s[par]
                for nh in range(2):
                    kb.op("T", lambda e, nh=nh, py=py: e.matmul(
                        py[:, nh * 512:(nh + 1) * 512], self.onesb_row[0:1, :],
                        self.b1bf[0:1, nh * 512:(nh + 1) * 512],
                        start=True, stop=False, skip_group_check=True),
                        reads=("b1bf", "onesbr"), writes=(f"py{par}",))
                for c in range(NC):
                    lhs = xit(par)[:, c * 128:(c + 1) * 128]
                    for nh in range(2):
                        kb.op("T", lambda e, c=c, nh=nh, lhs=lhs, py=py: e.matmul(
                            py[:, nh * 512:(nh + 1) * 512], lhs,
                            w1(c)[:, nh * 512:(nh + 1) * 512],
                            start=False, stop=(c == NC - 1), skip_group_check=True),
                            reads=(f"xt1_{par}", f"tw1{c}"), writes=(f"py{par}",))
                    lhsa = att(par)[:, c * 128:(c + 1) * 128]
                    for nh in range(2):
                        kb.op("T", lambda e, c=c, nh=nh, lhsa=lhsa, psn=psn: e.matmul(
                            psn[:, nh * 512:(nh + 1) * 512], lhsa,
                            wo(c)[:, nh * 512:(nh + 1) * 512],
                            start=(c == 0), stop=(c == NC - 1), skip_group_check=True),
                            reads=(f"at1_{par}", f"wo1{c}"), writes=(f"ps1{par}",))
                kb.op("S", lambda e, py=py: e.activation(
                    self.ga_f[:], py[:], AF.Silu, scale=self.bcsb[:, 3:4]),
                    reads=(f"py{par}", "s_bc"), writes=("rn_a",))
                kb.op("V", lambda e, psn=psn: e.tensor_tensor(
                    self.gb_f[:], psn[:], self.ga_f[:], ALU.mult),
                    reads=(f"ps1{par}", "rn_a"), writes=("rn_b",))
                kb.op("V", lambda e, par=par: e.tensor_tensor(
                    self.xn_f[:], self.xt(par)[:], self.gb_f[:], ALU.add),
                    reads=(f"xt{par}", "rn_b"), writes=("xn_f",))
                kb.dma("Y", "dout", lambda e, t=t: e.dma_start(
                    out=self.y_h[t], in_=self.xn_f[:]),
                    reads=("xn_f",), writes=("yout",))
            kb.flush()
        ar2.pop()


_CACHE = {}


def _host_prep(inputs):
    f = np.ascontiguousarray
    x = np.asarray(inputs["L0_attn_x"], dtype=np.float32)
    t = np.arange(S, dtype=np.float64)
    inv = 1.0 / (10000.0 ** (np.arange(0, HD1, 2, dtype=np.float64) / HD1))
    fr = np.outer(t, inv)
    emb = np.concatenate([fr, fr], axis=1)
    cos = np.cos(emb).astype(np.float32)
    sin = np.sin(emb).astype(np.float32)
    sinp = sin.copy()
    sinp[:, :HD1 // 2] *= -1.0
    common = {
        "wqkvT": f(np.asarray(inputs["L0_attn_w_qkv"], np.float32).T),
        "woT": f(np.asarray(inputs["L0_attn_w_out"], np.float32).T),
        "wgT": f(np.asarray(inputs["L0_ffn_w_gate"], np.float32).T),
        "wuT": f(np.asarray(inputs["L0_ffn_w_up"], np.float32).T),
        "wdT": f(np.asarray(inputs["L0_ffn_w_down"], np.float32).T),
        "wq1": f(np.asarray(inputs["L1_attn_Wq"], np.float32)),
        "wk1": f(np.asarray(inputs["L1_attn_Wk"], np.float32)),
        "wv1": f(np.asarray(inputs["L1_attn_Wv"], np.float32)),
        "wo1": f(np.asarray(inputs["L1_attn_Wo"], np.float32)),
        "ternT": f(np.asarray(inputs["L1_ffn_ternary_weight"], np.int32).T),
        "bias1": f(np.asarray(inputs["L1_ffn_bias"], np.float32).reshape(1, D)),
        "ascale": f(np.asarray(inputs["L1_ffn_act_scale"], np.float32).reshape(1, 1)),
        "identbf": np.eye(128, dtype=ml_dtypes.bfloat16),
        "onesf_col": np.ones((128, 1), np.float32),
        "onesf_row": np.ones((1, 128), np.float32),
        "onesb_row": np.ones((1, 128), ml_dtypes.bfloat16),
        "triubf": np.triu(np.ones((128, 128))).astype(ml_dtypes.bfloat16),
        "cosq": f(cos),
        "sinq": f(sinp),
        "econsts": np.array([[L0_EXP_SCALE, L0_EXP_BIAS, L1_EXP_SCALE,
                              -0.5, 0.5, 0.0, 0.0, 0.0]], np.float32),
    }
    per_core = []
    for b in range(8):
        m = dict(common)
        m["x0"] = f(x[b].reshape(NT, 128, D))
        per_core.append(m)
    return per_core


def kernel(**inputs):
    per_core = _host_prep(inputs)
    if "nc" not in _CACHE:
        _CACHE["nc"] = Builder(DEBUG_TAPS).build()
    nc = _CACHE["nc"]
    res = run_bass_kernel_spmd(nc, per_core, core_ids=list(range(8)))
    out = np.empty((8, S, D), np.float32)
    for b in range(8):
        out[b] = np.asarray(res.results[b]["y"]).reshape(S, D)
    return out


# version 09 (final)
# speedup vs baseline: 1.2415x; 1.2415x over previous
import numpy as np
import ml_dtypes
from contextlib import ExitStack

import concourse.bass as bass
import concourse.mybir as mybir
from concourse.bass_utils import run_bass_kernel_spmd

F32 = mybir.dt.float32
BF16 = mybir.dt.bfloat16
I32 = mybir.dt.int32
AF = mybir.ActivationFunctionType
ALU = mybir.AluOpType
AX = mybir.AxisListType

S = 4096
D = 1024
NT = S // 128
NC = D // 128
INNER = 4096
H0, HD0 = 16, 64
H1, HKV1, HD1 = 8, 4, 128
L0_EXP_SCALE = 0.125
L0_EXP_BIAS = -30.0
L1_EXP_SCALE = 0.08838834764831845
EPS = 1e-5

DEBUG_TAPS = ()


class KB:

    ENGS = ("T", "V", "S", "Y", "G")

    def __init__(self, nc, es):
        self.nc = nc
        self.es = es
        self.block = es.enter_context(nc.Block())
        self.sems = {}
        self.cnt = {}
        self.track = {}
        self.lastw = {}
        self.queue = {e: [] for e in self.ENGS}
        self.phase_tokens = {}
        for e in ("T", "V", "S"):
            self._mk_sem("c" + e)

    def _mk_sem(self, name):
        if name not in self.sems:
            self.sems[name] = self.es.enter_context(self.nc.semaphore(name))
            self.cnt[name] = 0
        return self.sems[name]

    def start_phase(self):
        self.phase_tokens = {s: v for s, v in self.cnt.items() if v > 0}

    def _get(self, k):
        t = self.track.get(k)
        if t is None:
            t = {"w": dict(self.phase_tokens), "r": {}}
            self.track[k] = t
        return t

    def _need(self, reads, writes):
        need = {}

        def add(tok):
            for s, v in tok.items():
                if s[0] == "d":
                    v = self.cnt[s]
                if need.get(s, 0) < v:
                    need[s] = v

        for k in reads:
            t = self._get(k)
            add(t["w"])
        for k in writes:
            t = self._get(k)
            add(t["w"])
            add(t["r"])
        return need

    def _wait(self, eng, need):
        for s, v in sorted(need.items()):
            if self.lastw.get((eng, s), -1) >= v:
                continue
            self.queue[eng].append(("wait", s, v))
            self.lastw[(eng, s)] = v

    def _update(self, reads, writes, sem, val):
        for k in writes:
            self.track[k] = {"w": {sem: val}, "r": {}}
        for k in reads:
            t = self.track.setdefault(k, {"w": {}, "r": {}})
            if t["r"].get(sem, 0) < val:
                t["r"][sem] = val

    def op(self, eng, fn, reads=(), writes=()):
        sem = "c" + eng
        self._wait(eng, self._need(reads, writes))
        self.cnt[sem] += 1
        val = self.cnt[sem]
        self.queue[eng].append(("inst", fn, sem, 1))
        self._update(reads, writes, sem, val)
        return (sem, val)

    def dma(self, eng, sem_name, fn, reads=(), writes=()):
        self._mk_sem(sem_name)
        self._wait(eng, self._need(reads, writes))
        self.cnt[sem_name] += 16
        val = self.cnt[sem_name]
        self.queue[eng].append(("inst", fn, sem_name, 16))
        self._update(reads, writes, sem_name, val)
        return (sem_name, val)

    def wait_token(self, eng, sem_name, val=None):
        if sem_name not in self.cnt:
            return
        v = self.cnt[sem_name] if val is None else val
        if self.lastw.get((eng, sem_name), -1) >= v:
            return
        self.queue[eng].append(("wait", sem_name, v))
        self.lastw[(eng, sem_name)] = v

    def barrier_key(self, key):
        self.track[key] = {"w": {s: v for s, v in self.cnt.items() if v > 0}, "r": {}}

    def flush(self):
        engmap = {e: q for e, q in self.queue.items() if q}
        self.queue = {e: [] for e in self.ENGS}

        def run(engine, items):
            for it in items:
                if it[0] == "wait":
                    engine.wait_ge(self.sems[it[1]], it[2])
                else:
                    _, fn, sem, inc = it
                    fn(engine).then_inc(self.sems[sem], inc)

        block = self.block
        if "Y" in engmap:
            @block.sync
            def _(sync, items=engmap["Y"]):
                run(sync, items)
        if "G" in engmap:
            @block.gpsimd
            def _(gpsimd, items=engmap["G"]):
                run(gpsimd, items)
        if "T" in engmap:
            @block.tensor
            def _(tensor, items=engmap["T"]):
                run(tensor, items)
        if "V" in engmap:
            @block.vector
            def _(vector, items=engmap["V"]):
                run(vector, items)
        if "S" in engmap:
            @block.scalar
            def _(scalar, items=engmap["S"]):
                run(scalar, items)


class Arena:
    def __init__(self, handle, cols):
        self.h = handle
        self.cols = cols
        self.off = 0
        self.marks = []

    def take(self, ncols, align=32):
        o = (self.off + align - 1) // align * align
        assert o + ncols <= self.cols, f"arena overflow {o}+{ncols}>{self.cols}"
        self.off = o + ncols
        return o

    def push(self):
        self.marks.append(self.off)

    def pop(self):
        self.off = self.marks.pop()


class Builder:
    def __init__(self, debug_taps=(), phases=None, dbg_raw=""):
        self.phases = phases
        self.dbg_raw = dbg_raw
        self.debug_taps = debug_taps
        self.nc = bass.Bass("TRN2", target_bir_lowering=False)
        self.dram = {}

    def din(self, name, shape, dt):
        ap = self.nc.dram_tensor(name, list(shape), dt, kind="ExternalInput").ap()
        self.dram[name] = ap
        return ap

    def dscr(self, name, shape, dt):
        kind = "ExternalOutput" if name in self.debug_taps else "Internal"
        ap = self.nc.dram_tensor(name, list(shape), dt, kind=kind).ap()
        self.dram[name] = ap
        return ap

    def sb(self, name, shape, dt=F32):
        return self.es.enter_context(self.nc.sbuf_tensor(name, list(shape), dt))

    def build(self):
        nc = self.nc
        self.x0_h = self.din("x0", (NT, 128, D), F32)
        self.wqkvT_h = self.din("wqkvT", (D, 3 * D), F32)
        self.woT_h = self.din("woT", (D, D), F32)
        self.wgT_h = self.din("wgT", (D, INNER), F32)
        self.wuT_h = self.din("wuT", (D, INNER), F32)
        self.wdT_h = self.din("wdT", (INNER, D), F32)
        self.wq1_h = self.din("wq1", (D, D), F32)
        self.wk1_h = self.din("wk1", (D, 512), F32)
        self.wv1_h = self.din("wv1", (D, 512), F32)
        self.wo1_h = self.din("wo1", (D, D), F32)
        self.ternT_h = self.din("ternT", (D, D), I32)
        self.bias_h = self.din("bias1", (1, D), F32)
        self.ascale_h = self.din("ascale", (1, 1), F32)
        self.ident_h = self.din("identbf", (128, 128), BF16)
        self.onesf_col_h = self.din("onesf_col", (128, 1), F32)
        self.onesf_row_h = self.din("onesf_row", (1, 128), F32)
        self.onesb_row_h = self.din("onesb_row", (1, 128), BF16)
        self.triu_h = self.din("triubf", (128, 128), BF16)
        self.cosq_h = self.din("cosq", (S, 128), F32)
        self.sinq_h = self.din("sinq", (S, 128), F32)
        self.econst_h = self.din("econsts", (1, 8), F32)

        self.y_h = nc.dram_tensor("y", [NT, 128, D], F32, kind="ExternalOutput").ap()
        self.dram["y"] = self.y_h

        self.q0_hbm = self.dscr("q0s", (NC, 128, S), BF16)
        self.k0_hbm = self.dscr("k0s", (NC, 128, S), BF16)
        self.xi_hbm = self.dscr("xis", (NT, 128, D), BF16)
        self.attn0_hbm = self.dscr("attn0s", (NC, 128, S), BF16)
        self.tg_hbm = self.dscr("tgs", (NC, 128, INNER), BF16)
        self.tu_hbm = self.dscr("tus", (NC, 128, INNER), BF16)
        self.td_hbm = self.dscr("tds", (INNER // 128, 128, D), BF16)
        self.xi2_hbm = self.dscr("xi2s", (NT, 128, INNER), BF16)
        self.x1_hbm = self.dscr("x1s", (NT, 128, D), F32)
        self.qr1_hbm = self.dscr("qr1s", (H1, 4, 128, 1024), BF16)
        self.xint_hbm = self.dscr("xints", (NT, 128, D), BF16)
        self.attn1_hbm = self.dscr("attn1s", (H1, 128, S), BF16)
        self.bc_hbm = self.dscr("bcs", (1, 16), F32)
        self.vdbg_hbm = self.dscr("v0dbg", (NT, 128, H0 * 65), BF16)
        self.g0dbg_hbm = self.dscr("g0dbg", (NT, 128, D), F32)
        self.s0dbg_hbm = self.dscr("s0dbg", (NT, 128, D), F32)

        es = ExitStack()
        self.es = es
        with es:
            self.kb = KB(nc, es)
            kb = self.kb

            AR2C, AR4C = 75296, 4608
            self.ar2h = self.sb("AR2", (128, AR2C), BF16)
            self.ar4h = self.sb("AR4", (128, AR4C), F32)
            self.ar2 = Arena(self.ar2h, AR2C)
            self.ar4 = Arena(self.ar4h, AR4C)
            self.rnd_i = self.sb("rndi", (128, 1024), I32)
            self.stats = self.sb("stats", (128, 32), F32)
            self.consts = self.sb("econsts_sb", (128, 8), F32)
            self.bcsb = self.sb("bcsb", (128, 16), F32)
            self.ident = self.sb("ident_sb", (128, 128), BF16)
            self.onesf_col = self.sb("onesf_col_sb", (128, 1), F32)
            self.onesf_row = self.sb("onesf_row_sb", (1, 128), F32)
            self.onesb_row = self.sb("onesb_row_sb", (1, 128), BF16)
            self.triu = self.sb("triu_sb", (128, 128), BF16)
            self.rl = self.sb("rl", (1, 1056), F32)
            self.b1row = self.sb("b1row", (1, D), F32)
            self.b1bf = self.sb("b1bf", (1, D), BF16)
            self.dqa = self.sb("dqa", (128, NT), F32)
            self.dq2a = self.sb("dq2a", (128, NT), F32)
            self.xt0 = self.sb("xt0", (128, 1024), F32)
            self.xt1 = self.sb("xt1", (128, 1024), F32)
            self.xn_f = self.sb("xnf_s", (128, 1024), F32)
            self.sq_f = self.sb("sqf_s", (128, 1024), F32)
            self.ga_f = self.sb("gaf_s", (128, 1024), F32)
            self.gb_f = self.sb("gbf_s", (128, 1024), F32)

            kb.dma("Y", "dcn", lambda e: e.dma_start(out=self.ident[:], in_=self.ident_h), writes=("ident",))
            kb.dma("Y", "dcn", lambda e: e.dma_start(out=self.onesf_col[:], in_=self.onesf_col_h), writes=("onesc",))
            kb.dma("Y", "dcn", lambda e: e.dma_start(out=self.onesf_row[:], in_=self.onesf_row_h), writes=("onesr",))
            kb.dma("Y", "dcn", lambda e: e.dma_start(out=self.onesb_row[:], in_=self.onesb_row_h), writes=("onesbr",))
            kb.dma("Y", "dcn", lambda e: e.dma_start(out=self.triu[:], in_=self.triu_h), writes=("triu",))
            kb.dma("Y", "dcn", lambda e: e.dma_start(
                out=self.consts[:], in_=self.econst_h[0:1, :].partition_broadcast(128)), writes=("econ",))
            kb.dma("Y", "dcn", lambda e: e.dma_start(out=self.b1row[:], in_=self.bias_h), writes=("b1row",))

            self.o_vaug = self.ar2.take(NT * H0 * 65)

            allp = ["p0l0", "pAl0", "pCl0", "pDl0", "p0l1", "pAl1", "pCl1"]
            want = self.phases if self.phases is not None else allp
            if "p0l0" in want: self.phase0_l0()
            if "pAl0" in want: self.phaseA_l0()
            self.ar2.off = self.o_vaug
            self.ar2.off = 0
            if "pCl0" in want: self.phaseC_l0()
            if "pDl0" in want: self.phaseD_l0()
            if "p0l1" in want: self.phase0_l1()
            if "pAl1" in want: self.phaseA_l1()
            if "pCl1" in want: self.phaseC_l1()

            for s in sorted(kb.cnt):
                if s[0] == "d":
                    kb.wait_token("Y", s)
            kb.flush()
        return nc

    def xt(self, p):
        return (self.xt0, self.xt1)[p]

    def emit_round(self, src_ap, rr_ap, out_ap, src_keys, dst_key, clip=None, n=1024,
                   sc=None):
        kb = self.kb
        ci = self.rnd_i[:, 0:n]
        dd = self.sq_f[:, 0:n]
        if sc is None:
            ga, gb, ka, kb_ = self.ga_f[:, 0:n], self.gb_f[:, 0:n], "rn_a", "rn_b"
        else:
            ga, gb, ka, kb_ = sc
        src_keys = tuple(src_keys)
        kb.op("S", lambda e: e.activation(ci, src_ap, AF.Identity, scale=rr_ap),
              reads=src_keys, writes=("rn_ci",))
        kb.op("V", lambda e: e.scalar_tensor_tensor(dd, src_ap, rr_ap, ci, ALU.mult, ALU.subtract),
              reads=src_keys + ("rn_ci",), writes=("rn_d",))
        kb.op("S", lambda e: e.activation(ga, dd, AF.Sign, bias=self.consts[:, 3:4]),
              reads=("rn_d", "econ"), writes=(ka,))
        kb.op("S", lambda e: e.activation(gb, dd, AF.Sign, bias=self.consts[:, 4:5]),
              reads=("rn_d", "econ"), writes=(kb_,))
        kb.op("V", lambda e: e.tensor_tensor(ga, ga, gb, ALU.add),
              reads=(ka, kb_), writes=(ka,))
        if clip is None:
            kb.op("V", lambda e: e.scalar_tensor_tensor(out_ap, ga, 0.5, ci, ALU.mult, ALU.add),
                  reads=(ka, "rn_ci"), writes=(dst_key,))
        else:
            kb.op("V", lambda e: e.scalar_tensor_tensor(dd, ga, 0.5, ci, ALU.mult, ALU.add),
                  reads=(ka, "rn_ci"), writes=("rn_d",))
            kb.op("V", lambda e: e.tensor_scalar(out_ap, dd, clip, -clip, ALU.min, ALU.max),
                  reads=("rn_d",), writes=(dst_key,))

    def emit_ln(self, x_ap, xkey, out_f32_ap, out_key):
        kb, st = self.kb, self.stats
        kb.op("V", lambda e: e.reduce_sum(st[:, 0:1], x_ap, axis=AX.X),
              reads=(xkey,), writes=("st_sum",))
        kb.op("S", lambda e: e.activation(self.sq_f[:], x_ap, AF.Square, accum_out=st[:, 1:2]),
              reads=(xkey,), writes=("rn_d", "st_sumsq"))
        kb.op("V", lambda e: e.tensor_single_scalar(st[:, 2:3], st[:, 0:1], 1.0 / D, ALU.mult),
              reads=("st_sum",), writes=("st_m",))
        kb.op("V", lambda e: e.tensor_single_scalar(st[:, 3:4], st[:, 1:2], 1.0 / D, ALU.mult),
              reads=("st_sumsq",), writes=("st_t",))
        kb.op("V", lambda e: e.tensor_tensor(st[:, 4:5], st[:, 2:3], st[:, 2:3], ALU.mult),
              reads=("st_m",), writes=("st_m2",))
        kb.op("V", lambda e: e.tensor_tensor(st[:, 5:6], st[:, 3:4], st[:, 4:5], ALU.subtract),
              reads=("st_t", "st_m2"), writes=("st_v",))
        kb.op("V", lambda e: e.tensor_scalar_add(st[:, 6:7], st[:, 5:6], EPS),
              reads=("st_v",), writes=("st_ve",))
        kb.op("V", lambda e: e.reciprocal(st[:, 7:8], st[:, 6:7]),
              reads=("st_ve",), writes=("st_rv",))
        kb.op("S", lambda e: e.activation(st[:, 8:9], st[:, 7:8], AF.Sqrt),
              reads=("st_rv",), writes=("st_r",))
        kb.op("V", lambda e: e.tensor_scalar(out_f32_ap, x_ap, st[:, 2:3], st[:, 8:9],
                                             ALU.subtract, ALU.mult),
              reads=(xkey, "st_m", "st_r"), writes=(out_key,))

    def emit_transpose(self, src_bf_ap, src_keys, dst_ap_fn, dst_key, nblk, evac_eng="V"):
        kb = self.kb
        for cb in range(nblk):
            half = cb % 2
            pslc = self.p_tr[:, half * 1024: half * 1024 + 128]
            src_slc = src_bf_ap[:, cb * 128:(cb + 1) * 128]
            dst_slc = dst_ap_fn(cb)
            kb.op("T", lambda e, src_slc=src_slc, pslc=pslc: e.transpose(
                pslc, src_slc, self.ident[:]),
                reads=tuple(src_keys) + ("ident",), writes=(f"ptr{half}",))
            if evac_eng == "V":
                kb.op("V", lambda e, dst_slc=dst_slc, pslc=pslc: e.tensor_copy(dst_slc, pslc),
                      reads=(f"ptr{half}",), writes=(dst_key,))
            else:
                kb.op("S", lambda e, dst_slc=dst_slc, pslc=pslc: e.activation(
                    dst_slc, pslc, AF.Identity),
                    reads=(f"ptr{half}",), writes=(dst_key,))

    def bcast_scalar(self, src_ap, src_keys, slot, dst_key):
        kb = self.kb
        kb.dma("G", "dbc", lambda e: e.dma_start(
            out=self.bc_hbm[0:1, slot:slot + 1], in_=src_ap),
            reads=tuple(src_keys), writes=(f"bch{slot}",))
        kb.dma("G", "dbc", lambda e: e.dma_start(
            out=self.bcsb[:, slot:slot + 1],
            in_=self.bc_hbm[0:1, slot:slot + 1].partition_broadcast(128)),
            reads=(f"bch{slot}",), writes=(dst_key,))

    def phase0_l0(self):
        kb, ar2, nc = self.kb, self.ar2, self.nc
        kb.start_phase()
        ar2.push()
        ar2h = self.ar2h
        o_wqkv = ar2.take(3 * D * NC)
        o_xnf = ar2.take(2 * NC * 512)
        o_xnb = ar2.take(1024)
        o_xib = ar2.take(1024)
        o_xist = ar2.take(1024)
        o_qkst = ar2.take(2 * 512)
        wqkv = lambda c, lo, hi: ar2h[:, o_wqkv + c * 3072 + lo: o_wqkv + c * 3072 + hi]
        xn_b = ar2h[:, o_xnb: o_xnb + 1024]
        xi_b = ar2h[:, o_xib: o_xib + 1024]
        xist = ar2h[:, o_xist: o_xist + 1024]
        qkst = lambda p: ar2h[:, o_qkst + p * 512: o_qkst + (p + 1) * 512]

        for c in range(NC):
            kb.dma("G", "dwq", lambda e, c=c: e.dma_start(
                out=wqkv(c, 0, 3072), in_=self.wqkvT_h[c * 128:(c + 1) * 128, :]),
                writes=(f"wqkv{c}",))

        with ExitStack() as ps:
            p_qk = ps.enter_context(nc.psum_tensor("p0_qk", [128, 1024], F32))
            p_v0 = ps.enter_context(nc.psum_tensor("p0_v0", [128, 1024], F32))
            p_v1 = ps.enter_context(nc.psum_tensor("p0_v1", [128, 1024], F32))
            self.p_tr = ps.enter_context(nc.psum_tensor("p0_tr", [128, 2048], BF16))

            for st8 in range(8):
                xnf = lambda c, lo, hi, st8=st8: ar2h[
                    :, o_xnf + (st8 % 2) * NC * 512 + c * 512 + lo:
                    o_xnf + (st8 % 2) * NC * 512 + c * 512 + hi]
                xnfkey = f"xnf{st8 % 2}"
                for sub in range(4):
                    t = st8 * 4 + sub
                    slot = t % 2
                    kb.dma("Y", f"dx{slot}", lambda e, t=t, slot=slot: e.dma_start(
                        out=self.xt(slot)[:], in_=self.x0_h[t]), writes=(f"xt{slot}",))
                    self.emit_ln(self.xt(slot)[:], f"xt{slot}", self.xn_f[:], "xn_f")
                    kb.op("V", lambda e: e.tensor_copy(xn_b[:], self.xn_f[:]),
                          reads=("xn_f",), writes=("xn_b",))
                    stc = self.stats
                    kb.op("V", lambda e: e.reduce_max(stc[:, 9:10], self.xn_f[:], axis=AX.X,
                                                      apply_absolute_value=True),
                          reads=("xn_f",), writes=("st_am",))
                    kb.op("V", lambda e: e.tensor_scalar_max(stc[:, 10:11], stc[:, 9:10], 1e-5),
                          reads=("st_am",), writes=("st_amc",))
                    kb.op("V", lambda e: e.reciprocal(stc[:, 11:12], stc[:, 10:11]),
                          reads=("st_amc",), writes=("st_ra",))
                    kb.op("V", lambda e: e.tensor_single_scalar(stc[:, 12:13], stc[:, 11:12],
                                                                127.0, ALU.mult),
                          reads=("st_ra",), writes=("st_rr",))
                    kb.op("V", lambda e, t=t: e.tensor_single_scalar(
                        self.dqa[:, t:t + 1], stc[:, 10:11], 1.0 / 127.0, ALU.mult),
                        reads=("st_amc",), writes=("dqa",))
                    self.emit_round(self.xn_f[:], stc[:, 12:13], xi_b,
                                    src_keys=("xn_f", "st_rr"), dst_key="xi_b")
                    self.emit_transpose(
                        xn_b, ("xn_b",),
                        lambda cb, sub=sub, st8=st8: xnf(cb, sub * 128, (sub + 1) * 128),
                        xnfkey, NC)
                    self.emit_transpose(xi_b, ("xi_b",),
                                        lambda cb: xist[:, cb * 128:(cb + 1) * 128],
                                        "xist", NC)
                    kb.dma("Y", "dxist", lambda e, t=t: e.dma_start(
                        out=self.xi_hbm[t], in_=xist[:]), reads=("xist",), writes=("xihbm",))
                    pv = p_v0 if (t % 2 == 0) else p_v1
                    pvkey = f"p0v{t % 2}"
                    for c in range(NC):
                        lhs = xnf(c, sub * 128, (sub + 1) * 128)
                        for nh in range(2):
                            kb.op("T", lambda e, c=c, nh=nh, lhs=lhs, pv=pv: e.matmul(
                                pv[:, nh * 512:(nh + 1) * 512], lhs,
                                wqkv(c, 2048 + nh * 512, 2048 + (nh + 1) * 512),
                                start=(c == 0), stop=(c == NC - 1), skip_group_check=True),
                                reads=(xnfkey, f"wqkv{c}"), writes=(pvkey,))
                    va = ar2h[:, self.o_vaug + t * (H0 * 65):
                              self.o_vaug + (t + 1) * (H0 * 65)].rearrange(
                                  "p (h c) -> p h c", h=H0)
                    kb.op("V", lambda e, va=va, pv=pv: e.tensor_copy(
                        va[:, :, 1:65], pv[:].rearrange("p (h d) -> p h d", h=H0)),
                        reads=(pvkey,), writes=("vaug",))
                    kb.op("V", lambda e, va=va: e.memset(va[:, :, 0:1], 1.0),
                          writes=("vaug",))
                    if "v0dbg" in self.debug_taps:
                        kb.dma("Y", "dsp", lambda e, t=t: e.dma_start(
                            out=self.vdbg_hbm[t],
                            in_=self.ar2h[:, self.o_vaug + t * (H0 * 65):
                                          self.o_vaug + (t + 1) * (H0 * 65)]),
                            reads=("vaug",), writes=("vdbg",))
                for o in range(16):
                    pq = p_qk[:, (o % 2) * 512:(o % 2) * 512 + 512]
                    pqkey = f"p0qk{o % 2}"
                    for c in range(NC):
                        lhsw = wqkv(c, o * 128, (o + 1) * 128)
                        rhsx = xnf(c, 0, 512)
                        kb.op("T", lambda e, lhsw=lhsw, rhsx=rhsx, pq=pq, c=c: e.matmul(
                            pq, lhsw, rhsx,
                            start=(c == 0), stop=(c == NC - 1), skip_group_check=True),
                            reads=(xnfkey, f"wqkv{c}"), writes=(pqkey,))
                    qs = qkst(o % 2)
                    kb.op("V", lambda e, pq=pq, qs=qs: e.tensor_copy(qs, pq),
                          reads=(pqkey,), writes=(f"qkst{o % 2}",))
                    hbm = self.q0_hbm if o < 8 else self.k0_hbm
                    kb.dma("Y", f"dqk{o % 2}", lambda e, hbm=hbm, c=o % 8, st8=st8, qs=qs: e.dma_start(
                        out=hbm[c][:, st8 * 512:(st8 + 1) * 512], in_=qs),
                        reads=(f"qkst{o % 2}",), writes=("qkhbm",))
            kb.flush()
        ar2.pop()

    def phaseA_l0(self):
        kb, ar2, ar4, nc = self.kb, self.ar2, self.ar4, self.nc
        kb.start_phase()
        ar2.push()
        ar4.push()
        ar2h, ar4h = self.ar2h, self.ar4h
        o_kf = ar2.take(NC * S)
        o_qb = ar2.take(2 * 1024)
        o_pab = ar2.take(2 * 2048)
        o_sta = ar2.take(1024)
        o_stb = ar2.take(1024)
        o_tst = ar2.take(1024)
        kf = lambda c: ar2h[:, o_kf + c * S: o_kf + (c + 1) * S]
        qb = lambda p: ar2h[:, o_qb + p * 1024: o_qb + (p + 1) * 1024]
        pab = lambda par: ar2h[:, o_pab + par * 2048: o_pab + (par + 1) * 2048]
        stA = ar2h[:, o_sta: o_sta + 1024]
        stB = ar2h[:, o_stb: o_stb + 1024]
        self.tstage = ar2h[:, o_tst: o_tst + 1024]
        vaug = lambda t, h: ar2h[:, self.o_vaug + t * (H0 * 65) + h * 65:
                                 self.o_vaug + t * (H0 * 65) + (h + 1) * 65]
        o_wb = ar4.take(4096)
        wb = lambda p: ar4h[:, o_wb: o_wb + 4096]

        for c in range(NC):
            kb.dma("Y", "dkf", lambda e, c=c: e.dma_start(out=kf(c), in_=self.k0_hbm[c]),
                   reads=("qkhbm",), writes=(f"kf{c}",))

        with ExitStack() as ps:
            psS = ps.enter_context(nc.psum_tensor("pa_s", [128, 2048], F32))
            psO_A = ps.enter_context(nc.psum_tensor("pa_oa", [128, 1024], F32))
            psO_B = ps.enter_context(nc.psum_tensor("pa_ob", [128, 1024], F32))

            def emit_pv(p, j):
                pj = pab(j % 2)
                for nh in range(2):
                    kb.op("T", lambda e, p=p, j=j, nh=nh, pj=pj: e.matmul(
                        psO_A[0:65, nh * 512:(nh + 1) * 512],
                        vaug(j, 2 * p), pj[:, nh * 512:(nh + 1) * 512],
                        start=(j == 0), stop=(j == NT - 1), skip_group_check=True),
                        reads=("vaug", f"pAB{j % 2}"), writes=("psOA",))
                for nh in range(2):
                    kb.op("T", lambda e, p=p, j=j, nh=nh, pj=pj: e.matmul(
                        psO_B[0:65, nh * 512:(nh + 1) * 512],
                        vaug(j, 2 * p + 1), pj[:, 1024 + nh * 512: 1024 + (nh + 1) * 512],
                        start=(j == 0), stop=(j == NT - 1), skip_group_check=True),
                        reads=("vaug", f"pAB{j % 2}"), writes=("psOB",))

            tern = self.gen_ternarize_l0(wb, psS)
            for p in range(8):
                for qc in range(4):
                    par = (p * 4 + qc) % 2
                    kb.dma("Y", f"dqb{par}", lambda e, p=p, qc=qc, par=par: e.dma_start(
                        out=qb(par), in_=self.q0_hbm[p][:, qc * 1024:(qc + 1) * 1024]),
                        reads=("qkhbm",), writes=(f"qb{par}",))
                    for j in range(NT):
                        for nh in range(2):
                            kb.op("T", lambda e, p=p, j=j, nh=nh, par=par: e.matmul(
                                psS[:, nh * 512:(nh + 1) * 512],
                                kf(p)[0:64, j * 128:(j + 1) * 128],
                                qb(par)[0:64, nh * 512:(nh + 1) * 512],
                                start=True, stop=True, skip_group_check=True),
                                reads=(f"kf{p}", f"qb{par}"), writes=("psSa",))
                        kb.op("S", lambda e, j=j: e.activation(
                            pab(j % 2)[:, 0:1024], psS[:, 0:1024], AF.Exp,
                            scale=self.consts[:, 0:1], bias=self.consts[:, 1:2]),
                            reads=("psSa", "econ"), writes=(f"pAB{j % 2}",))
                        for nh in range(2):
                            kb.op("T", lambda e, p=p, j=j, nh=nh, par=par: e.matmul(
                                psS[:, 1024 + nh * 512: 1024 + (nh + 1) * 512],
                                kf(p)[64:128, j * 128:(j + 1) * 128],
                                qb(par)[64:128, nh * 512:(nh + 1) * 512],
                                start=True, stop=True, skip_group_check=True,
                                tile_position=(64, 0)),
                                reads=(f"kf{p}", f"qb{par}"), writes=("psSb",))
                        kb.op("S", lambda e, j=j: e.activation(
                            pab(j % 2)[:, 1024:2048], psS[:, 1024:2048], AF.Exp,
                            scale=self.consts[:, 0:1], bias=self.consts[:, 1:2]),
                            reads=("psSb", "econ"), writes=(f"pAB{j % 2}",))
                        if j > 0:
                            emit_pv(p, j - 1)
                    emit_pv(p, NT - 1)
                    kb.op("V", lambda e: e.reciprocal(self.rl[0:1, 0:1024], psO_A[0:1, :]),
                          reads=("psOA",), writes=("rlX",))
                    kb.op("V", lambda e: e.tensor_copy(stA[0:65, :], psO_A[0:65, :]),
                          reads=("psOA",), writes=("stA",))
                    for nh in range(2):
                        kb.op("T", lambda e, nh=nh: e.matmul(
                            psO_A[0:65, nh * 512:(nh + 1) * 512],
                            self.onesf_row[0:1, 0:65],
                            self.rl[0:1, nh * 512:(nh + 1) * 512],
                            start=True, stop=True, skip_group_check=True),
                            reads=("rlX", "onesr"), writes=("psOA",))
                    kb.op("V", lambda e: e.tensor_tensor(stA[0:65, :], stA[0:65, :],
                                                         psO_A[0:65, :], ALU.mult),
                          reads=("stA", "psOA"), writes=("stA",))
                    kb.op("V", lambda e: e.reciprocal(self.rl[0:1, 0:1024], psO_B[0:1, :]),
                          reads=("psOB", "rlX"), writes=("rlX",))
                    kb.op("V", lambda e: e.tensor_copy(stB[0:65, :], psO_B[0:65, :]),
                          reads=("psOB",), writes=("stB",))
                    for nh in range(2):
                        kb.op("T", lambda e, nh=nh: e.matmul(
                            psO_B[0:65, nh * 512:(nh + 1) * 512],
                            self.onesf_row[0:1, 0:65],
                            self.rl[0:1, nh * 512:(nh + 1) * 512],
                            start=True, stop=True, skip_group_check=True),
                            reads=("rlX", "onesr"), writes=("psOB",))
                    kb.op("V", lambda e: e.tensor_tensor(stB[0:65, :], stB[0:65, :],
                                                         psO_B[0:65, :], ALU.mult),
                          reads=("stB", "psOB"), writes=("stB",))
                    kb.dma("Y", "dstA", lambda e, p=p, qc=qc: e.dma_start(
                        out=self.attn0_hbm[p][0:64, qc * 1024:(qc + 1) * 1024],
                        in_=stA[1:65, :]), reads=("stA",), writes=("attn0hbm",))
                    kb.dma("Y", "dstB", lambda e, p=p, qc=qc: e.dma_start(
                        out=self.attn0_hbm[p][64:128, qc * 1024:(qc + 1) * 1024],
                        in_=stB[1:65, :]), reads=("stB",), writes=("attn0hbm",))
                    next(tern, None)
            for _ in tern:
                pass
            kb.flush()
        ar2.pop()
        ar4.pop()

    def gen_ternarize_l0(self, wb, p_s):
        kb = self.kb
        stc = self.stats
        for wi, (w_h, t_hbm, nch, wcols, slot) in enumerate([
                (self.wgT_h, self.tg_hbm, NC, INNER, 0),
                (self.wuT_h, self.tu_hbm, NC, INNER, 1),
                (self.wdT_h, self.td_hbm, INNER // 128, D, 2)]):
            nw = float(nch * 128 * wcols)
            for c in range(nch):
                kb.dma("G", "dwb", lambda e, c=c, w_h=w_h, wcols=wcols: e.dma_start(
                    out=wb(0)[:, 0:wcols], in_=w_h[c * 128:(c + 1) * 128, :]),
                    writes=("wb0",))
                kb.op("V", lambda e, wcols=wcols: e.reduce_sum(
                    stc[:, 17:18], wb(0)[:, 0:wcols], axis=AX.X,
                    apply_absolute_value=True),
                    reads=("wb0",), writes=("st_ws1",))
                if c == 0:
                    kb.op("V", lambda e: e.tensor_copy(stc[:, 16:17], stc[:, 17:18]),
                          reads=("st_ws1",), writes=("st_ws",))
                else:
                    kb.op("V", lambda e: e.tensor_tensor(stc[:, 16:17], stc[:, 16:17],
                                                         stc[:, 17:18], ALU.add),
                          reads=("st_ws1", "st_ws"), writes=("st_ws",))
                yield
            kb.op("T", lambda e, slot=slot: e.matmul(
                p_s[0:1, slot:slot + 1], stc[:, 16:17], self.onesf_col[:],
                start=True, stop=True, skip_group_check=True),
                reads=("st_ws", "onesc"), writes=("psSa",))
            pos = 1024 + slot
            kb.op("V", lambda e, slot=slot, nw=nw, pos=pos: e.tensor_scalar(
                self.rl[0:1, pos:pos + 1], p_s[0:1, slot:slot + 1],
                1.0 / nw, 1e-5, ALU.mult, ALU.max),
                reads=("psSa",), writes=(f"wmc{slot}",))
            self.bcast_scalar(self.rl[0:1, pos:pos + 1], (f"wmc{slot}",),
                              slot, f"wmean{slot}")
            kb.op("V", lambda e, slot=slot: e.reciprocal(
                self.bcsb[:, 8 + slot:9 + slot], self.bcsb[:, slot:slot + 1]),
                reads=(f"wmean{slot}",), writes=(f"wrec{slot}",))
            yield
            for c in range(nch):
                kb.dma("G", "dwb", lambda e, c=c, w_h=w_h, wcols=wcols: e.dma_start(
                    out=wb(0)[:, 0:wcols], in_=w_h[c * 128:(c + 1) * 128, :]),
                    writes=("wb0",))
                for seg in range(wcols // 1024):
                    sl = slice(seg * 1024, (seg + 1) * 1024)
                    kb.op("V", lambda e, sl=sl, slot=slot: e.tensor_scalar(
                        self.ga_f[:], wb(0)[:, sl], self.bcsb[:, 8 + slot:9 + slot],
                        0.5, ALU.mult, ALU.is_gt),
                        reads=("wb0", f"wrec{slot}"), writes=("rn_a",))
                    kb.op("V", lambda e, sl=sl, slot=slot: e.tensor_scalar(
                        self.gb_f[:], wb(0)[:, sl], self.bcsb[:, 8 + slot:9 + slot],
                        -0.5, ALU.mult, ALU.is_lt),
                        reads=("wb0", f"wrec{slot}"), writes=("rn_b",))
                    kb.op("V", lambda e: e.tensor_tensor(
                        self.tstage[:], self.ga_f[:], self.gb_f[:], ALU.subtract),
                        reads=("rn_a", "rn_b"), writes=("tstage",))
                    kb.dma("Y", "dtst", lambda e, c=c, seg=seg, t_hbm=t_hbm: e.dma_start(
                        out=t_hbm[c][:, seg * 1024:(seg + 1) * 1024], in_=self.tstage[:]),
                        reads=("tstage",), writes=("ternhbm",))
                yield

    def phaseC_l0(self):
        kb, ar2, ar4, nc = self.kb, self.ar2, self.ar4, self.nc
        kb.start_phase()
        ar2.push()
        ar4.push()
        ar2h, ar4h = self.ar2h, self.ar4h
        o_wg = ar2.take(NC * INNER)
        o_wu = ar2.take(NC * INNER)
        o_xit = ar2.take(2 * 1024)
        o_x2b = ar2.take(2048)
        o_x2st = ar2.take(1024)
        o_sil = ar2.take(1024)
        o_sca = ar2.take(2 * 1024)
        wg = lambda c: ar2h[:, o_wg + c * INNER: o_wg + (c + 1) * INNER]
        wu = lambda c: ar2h[:, o_wu + c * INNER: o_wu + (c + 1) * INNER]
        xit = lambda p: ar2h[:, o_xit + p * 1024: o_xit + (p + 1) * 1024]
        xi2_b = lambda s: ar2h[:, o_x2b + s * 1024: o_x2b + (s + 1) * 1024]
        xi2st = ar2h[:, o_x2st: o_x2st + 1024]
        sil_b = ar2h[:, o_sil: o_sil + 1024]
        sca = ar2h[:, o_sca: o_sca + 1024]
        scb = ar2h[:, o_sca + 1024: o_sca + 2048]
        o_g2 = ar4.take(4096)
        g2 = ar4h[:, o_g2: o_g2 + 4096]

        for c in range(NC):
            kb.dma("G", "dwg", lambda e, c=c: e.dma_start(out=wg(c), in_=self.tg_hbm[c]),
                   reads=("ternhbm",), writes=(f"twg{c}",))
            kb.dma("G", "dwg", lambda e, c=c: e.dma_start(out=wu(c), in_=self.tu_hbm[c]),
                   reads=("ternhbm",), writes=(f"twu{c}",))

        with ExitStack() as ps:
            p_g = [ps.enter_context(nc.psum_tensor("pc_g0", [128, 1024], F32)),
                   ps.enter_context(nc.psum_tensor("pc_g1", [128, 1024], F32))]
            p_u0 = ps.enter_context(nc.psum_tensor("pc_u0", [128, 1024], F32))
            p_u = [p_u0, p_u0]
            self.p_tr = ps.enter_context(nc.psum_tensor("pc_tr", [128, 2048], BF16))
            stc = self.stats

            def emit_seg_round(t, seg):
                sl = slice(seg * 1024, (seg + 1) * 1024)
                self.emit_round(g2[:, sl], stc[:, 12:13], xi2_b(seg % 2),
                                src_keys=(f"g2i{seg}", "st_rr"), dst_key=f"x2b{seg % 2}",
                                sc=(sca[:], scb[:], "rc_a", "rc_b"))

            def emit_seg_spill(t, seg):
                self.emit_transpose(xi2_b(seg % 2), (f"x2b{seg % 2}",),
                                    lambda cb: xi2st[:, cb * 128:(cb + 1) * 128],
                                    "xi2st", 8, evac_eng="S")
                kb.dma("Y", "dx2st", lambda e, t=t, seg=seg: e.dma_start(
                    out=self.xi2_hbm[t][:, seg * 1024:(seg + 1) * 1024], in_=xi2st[:]),
                    reads=("xi2st",), writes=("xi2hbm",))

            def emit_combine(t):
                kb.op("V", lambda e: e.tensor_tensor(stc[:, 17:18], stc[:, 13:14],
                                                     stc[:, 14:15], ALU.max),
                      reads=("pm0", "pm1"), writes=("cmb0",))
                kb.op("V", lambda e: e.tensor_tensor(stc[:, 18:19], stc[:, 15:16],
                                                     stc[:, 16:17], ALU.max),
                      reads=("pm2", "pm3"), writes=("cmb1",))
                kb.op("V", lambda e: e.tensor_tensor(stc[:, 9:10], stc[:, 17:18],
                                                     stc[:, 18:19], ALU.max),
                      reads=("cmb0", "cmb1"), writes=("st_am",))
                kb.op("V", lambda e: e.tensor_scalar_max(stc[:, 10:11], stc[:, 9:10], 1e-5),
                      reads=("st_am",), writes=("st_amc",))
                kb.op("V", lambda e: e.reciprocal(stc[:, 11:12], stc[:, 10:11]),
                      reads=("st_amc",), writes=("st_ra",))
                kb.op("V", lambda e: e.tensor_single_scalar(stc[:, 12:13], stc[:, 11:12],
                                                            127.0, ALU.mult),
                      reads=("st_ra",), writes=("st_rr",))
                kb.op("V", lambda e, t=t: e.tensor_single_scalar(
                    self.dq2a[:, t:t + 1], stc[:, 10:11], 1.0 / 127.0, ALU.mult),
                    reads=("st_amc",), writes=("dq2a",))

            for t in range(NT):
                par = t % 2
                kb.dma("Y", f"dxi{par}", lambda e, t=t, par=par: e.dma_start(
                    out=xit(par), in_=self.xi_hbm[t]),
                    reads=("xihbm",), writes=(f"xit{par}",))
                for icp in range(4):
                    pp = icp % 2
                    pg, pu = p_g[pp], p_u[0]
                    for c in range(NC):
                        lhs = xit(par)[:, c * 128:(c + 1) * 128]
                        for nh in range(2):
                            kb.op("T", lambda e, c=c, nh=nh, icp=icp, lhs=lhs, pg=pg: e.matmul(
                                pg[:, nh * 512:(nh + 1) * 512], lhs,
                                wg(c)[:, icp * 1024 + nh * 512: icp * 1024 + (nh + 1) * 512],
                                start=(c == 0), stop=(c == NC - 1), skip_group_check=True),
                                reads=(f"xit{par}", f"twg{c}"), writes=(f"pcg{pp}",))
                            kb.op("T", lambda e, c=c, nh=nh, icp=icp, lhs=lhs, pu=pu: e.matmul(
                                pu[:, nh * 512:(nh + 1) * 512], lhs,
                                wu(c)[:, icp * 1024 + nh * 512: icp * 1024 + (nh + 1) * 512],
                                start=(c == 0), stop=(c == NC - 1), skip_group_check=True),
                                reads=(f"xit{par}", f"twu{c}"), writes=("pcu",))
                    if t > 0:
                        emit_seg_round(t - 1, icp)
                    kb.op("V", lambda e, t=t: e.tensor_scalar(
                        self.stats[:, 20:21], self.dqa[:, t:t + 1],
                        self.bcsb[:, 0:1], None, ALU.mult),
                        reads=("dqa", "wmean0"), writes=("st_sg",))
                    kb.op("V", lambda e, t=t: e.tensor_scalar(
                        self.stats[:, 21:22], self.dqa[:, t:t + 1],
                        self.bcsb[:, 1:2], None, ALU.mult),
                        reads=("dqa", "wmean1"), writes=("st_su",))
                    kb.op("S", lambda e, pg=pg, pp=pp: e.activation(
                        sil_b[:], pg[:], AF.Silu, scale=self.stats[:, 20:21]),
                        reads=(f"pcg{pp}", "st_sg"), writes=("silb",))
                    kb.op("V", lambda e, pu=pu, icp=icp: e.scalar_tensor_tensor(
                        g2[:, icp * 1024:(icp + 1) * 1024], pu[:], self.stats[:, 21:22],
                        sil_b[:], ALU.mult, ALU.mult),
                        reads=("pcu", "st_su", "silb"), writes=(f"g2i{icp}",))
                    kb.op("V", lambda e, icp=icp: e.reduce_max(
                        stc[:, 13 + icp:14 + icp], g2[:, icp * 1024:(icp + 1) * 1024],
                        axis=AX.X, apply_absolute_value=True),
                        reads=(f"g2i{icp}",), writes=(f"pm{icp}",))
                    if t > 0 and icp >= 1:
                        emit_seg_spill(t - 1, icp - 1)
                if t > 0:
                    emit_seg_spill(t - 1, 3)
                emit_combine(t)
            for seg in range(4):
                emit_seg_round(NT - 1, seg)
                emit_seg_spill(NT - 1, seg)
            kb.flush()
        ar2.pop()
        ar4.pop()

    def phaseD_l0(self):
        kb, ar2, nc = self.kb, self.ar2, self.nc
        ar2.push()
        ar2h = self.ar2h
        o_wd = ar2.take(32 * D)
        o_wo = ar2.take(NC * D)
        o_x2t = ar2.take(2 * 4096)
        o_att = ar2.take(2 * 8192)
        wd = lambda c: ar2h[:, o_wd + c * D: o_wd + (c + 1) * D]
        wo = lambda c: ar2h[:, o_wo + c * D: o_wo + (c + 1) * D]
        x2t = lambda p: ar2h[:, o_x2t + p * 4096: o_x2t + (p + 1) * 4096]
        attsp = lambda p: ar2h[:, o_att + p * 8192: o_att + (p + 1) * 8192]
        kb.start_phase()

        for c in range(32):
            kb.dma("G", "dwd", lambda e, c=c: e.dma_start(out=wd(c), in_=self.td_hbm[c]),
                   reads=("ternhbm",), writes=(f"twd{c}",))
        for c in range(NC):
            kb.dma("G", "dwd", lambda e, c=c: e.dma_start(
                out=wo(c), in_=self.woT_h[c * 128:(c + 1) * 128, :]),
                writes=(f"wo{c}",))

        with ExitStack() as ps:
            p_d = [ps.enter_context(nc.psum_tensor(f"pd_d{i}", [128, 1024], F32))
                   for i in range(2)]
            p_o = [ps.enter_context(nc.psum_tensor(f"pd_o{i}", [128, 1024], F32))
                   for i in range(2)]

            for t in range(NT):
                par = t % 2
                kb.dma("Y", f"dxi{par}", lambda e, t=t, par=par: e.dma_start(
                    out=x2t(par), in_=self.xi2_hbm[t]),
                    reads=("xi2hbm",), writes=(f"x2t{par}",))
                if t % 8 == 0:
                    sc = t // 8
                    kb.dma("Y", f"dat{sc % 2}", lambda e, sc=sc: e.dma_start(
                        out=attsp(sc % 2).rearrange("p (c w) -> p c w", c=NC),
                        in_=self.attn0_hbm[:, :, sc * 1024:(sc + 1) * 1024]
                            .transpose([1, 0, 2])),
                        reads=("attn0hbm",), writes=(f"att{sc % 2}",))
                kb.dma("Y", f"dx{par}", lambda e, t=t, par=par: e.dma_start(
                    out=self.xt(par)[:], in_=self.x0_h[t]), writes=(f"xt{par}",))
                pd, po = p_d[par], p_o[par]
                for c in range(32):
                    lhs = x2t(par)[:, c * 128:(c + 1) * 128]
                    for nh in range(2):
                        kb.op("T", lambda e, c=c, nh=nh, lhs=lhs, pd=pd: e.matmul(
                            pd[:, nh * 512:(nh + 1) * 512], lhs,
                            wd(c)[:, nh * 512:(nh + 1) * 512],
                            start=(c == 0), stop=(c == 31), skip_group_check=True),
                            reads=(f"x2t{par}", f"twd{c}"), writes=(f"pdd{par}",))
                scp = (t // 8) % 2
                for c in range(NC):
                    lhs = attsp(scp)[:, c * 1024 + (t % 8) * 128:
                                     c * 1024 + (t % 8) * 128 + 128]
                    for nh in range(2):
                        kb.op("T", lambda e, c=c, nh=nh, lhs=lhs, po=po: e.matmul(
                            po[:, nh * 512:(nh + 1) * 512], lhs,
                            wo(c)[:, nh * 512:(nh + 1) * 512],
                            start=(c == 0), stop=(c == NC - 1), skip_group_check=True),
                            reads=(f"att{scp}", f"wo{c}"), writes=(f"pdo{par}",))
                kb.op("V", lambda e, t=t: e.tensor_scalar(
                    self.stats[:, 22:23], self.dq2a[:, t:t + 1],
                    self.bcsb[:, 2:3], None, ALU.mult),
                    reads=("dq2a", "wmean2"), writes=("st_sd",))
                if self.dbg_raw == "pd":
                    kb.op("S", lambda e, pd=pd: e.activation(
                        self.ga_f[:], pd[:], AF.Silu),
                        reads=(f"pdd{par}",), writes=("rn_a",))
                else:
                    kb.op("S", lambda e, pd=pd: e.activation(
                        self.ga_f[:], pd[:], AF.Silu, scale=self.stats[:, 22:23]),
                        reads=(f"pdd{par}", "st_sd"), writes=("rn_a",))
                if "g0dbg" in self.debug_taps:
                    kb.dma("Y", "dsp", lambda e, t=t: e.dma_start(
                        out=self.g0dbg_hbm[t], in_=self.ga_f[:]),
                        reads=("rn_a",), writes=("g0dbg",))
                    kb.op("V", lambda e, po=po: e.tensor_copy(self.sq_f[:], po[:]),
                          reads=(f"pdo{par}",), writes=("rn_d",))
                    kb.dma("Y", "dsp", lambda e, t=t: e.dma_start(
                        out=self.s0dbg_hbm[t], in_=self.sq_f[:]),
                        reads=("rn_d",), writes=("s0dbg",))
                kb.op("V", lambda e, po=po: e.tensor_tensor(
                    self.gb_f[:], po[:], self.ga_f[:], ALU.mult),
                    reads=(f"pdo{par}", "rn_a"), writes=("rn_b",))
                kb.op("V", lambda e, par=par: e.tensor_tensor(
                    self.xn_f[:], self.xt(par)[:], self.gb_f[:], ALU.add),
                    reads=(f"xt{par}", "rn_b"), writes=("xn_f",))
                kb.dma("Y", "dx1s", lambda e, t=t: e.dma_start(
                    out=self.x1_hbm[t], in_=self.xn_f[:]),
                    reads=("xn_f",), writes=("x1hbm",))
            kb.flush()
        ar2.pop()

    def phase0_l1(self):
        kb, ar2, ar4, nc = self.kb, self.ar2, self.ar4, self.nc
        ar2.push()
        ar4.push()
        ar2h, ar4h = self.ar2h, self.ar4h
        o_wq = ar2.take(NC * D)
        o_wk = ar2.take(NC * 512)
        o_wv = ar2.take(NC * 512)
        self.o_k1 = ar2.take(HKV1 * S)
        self.o_v1 = ar2.take(NT * HKV1 * 129)
        o_xnb = ar2.take(2 * 1024)
        o_qrb = ar2.take(2 * 1024)
        o_krb = ar2.take(2 * 512)
        o_xib = ar2.take(2 * 1024)
        o_sxn = ar2.take(2 * 1024)
        o_qsp = ar2.take(8192)
        o_sti = ar2.take(2 * 1024)
        wq1 = lambda c: ar2h[:, o_wq + c * D: o_wq + (c + 1) * D]
        wk1 = lambda c: ar2h[:, o_wk + c * 512: o_wk + (c + 1) * 512]
        wv1 = lambda c: ar2h[:, o_wv + c * 512: o_wv + (c + 1) * 512]
        xn_b = lambda p: ar2h[:, o_xnb + p * 1024: o_xnb + (p + 1) * 1024]
        qr_b = lambda p: ar2h[:, o_qrb + p * 1024: o_qrb + (p + 1) * 1024]
        kr_b = lambda p: ar2h[:, o_krb + p * 512: o_krb + (p + 1) * 512]
        xi_b = lambda p: ar2h[:, o_xib + p * 1024: o_xib + (p + 1) * 1024]
        sxn = lambda p: ar2h[:, o_sxn + p * 1024: o_sxn + (p + 1) * 1024]
        qsp = ar2h[:, o_qsp: o_qsp + 8192]
        sti = lambda p: ar2h[:, o_sti + p * 1024: o_sti + (p + 1) * 1024]
        o_tab = ar4.take(2 * 256)
        tab = lambda p: ar4h[:, o_tab + p * 256: o_tab + (p + 1) * 256]
        o_qkf = ar4.take(1536)
        qkf = ar4h[:, o_qkf: o_qkf + 1536]
        o_rp = ar4.take(2048)
        rp_a = ar4h[:, o_rp: o_rp + 1024]
        rp_b = ar4h[:, o_rp + 1024: o_rp + 2048]
        kb.start_phase()

        for c in range(NC):
            kb.dma("G", "dw1a", lambda e, c=c: e.dma_start(
                out=wq1(c), in_=self.wq1_h[c * 128:(c + 1) * 128, :]), writes=(f"w1q{c}",))
            kb.dma("G", "dw1a", lambda e, c=c: e.dma_start(
                out=wk1(c), in_=self.wk1_h[c * 128:(c + 1) * 128, :]), writes=(f"w1k{c}",))
            kb.dma("G", "dw1a", lambda e, c=c: e.dma_start(
                out=wv1(c), in_=self.wv1_h[c * 128:(c + 1) * 128, :]), writes=(f"w1v{c}",))

        kb.dma("Y", "dasc", lambda e: e.dma_start(out=self.rl[0:1, 1028:1029], in_=self.ascale_h),
               writes=("asc_raw",))
        kb.op("V", lambda e: e.tensor_scalar_max(self.rl[0:1, 1029:1030],
                                                 self.rl[0:1, 1028:1029], 1e-5),
              reads=("asc_raw",), writes=("asc_s",))
        self.bcast_scalar(self.rl[0:1, 1029:1030], ("asc_s",), 3, "s_bc")
        kb.op("V", lambda e: e.reciprocal(self.bcsb[:, 8 + 3:8 + 4], self.bcsb[:, 3:4]),
              reads=("s_bc",), writes=("rs_bc",))
        kb.op("V", lambda e: e.tensor_single_scalar(
            self.b1bf[0:1, :], self.b1row[0:1, :], self.bcsb[0:1, 8 + 3:8 + 4], ALU.mult),
            reads=("b1row", "rs_bc"), writes=("b1bf",))

        with ExitStack() as ps:
            p_q = [ps.enter_context(nc.psum_tensor(f"p1_q{i}", [128, 1024], F32))
                   for i in range(2)]
            p_kv0 = ps.enter_context(nc.psum_tensor("p1_kv0", [128, 1024], F32))
            p_kv = [p_kv0, p_kv0]
            self.p_tr = ps.enter_context(nc.psum_tensor("p1_tr", [128, 2048], BF16))

            def emit_pre(t):
                par = t % 2
                pq, pkv = p_q[par], p_kv[par]
                kb.op("V", lambda e, pq=pq: e.tensor_copy(qkf[:, 0:1024], pq[:]),
                      reads=(f"p1q{par}",), writes=("qkvf",))
                kb.op("V", lambda e, pkv=pkv: e.tensor_copy(qkf[:, 1024:1536], pkv[:, 0:512]),
                      reads=("p1kv",), writes=("qkvf",))
                va = ar2h[:, self.o_v1 + t * (HKV1 * 129):
                          self.o_v1 + (t + 1) * (HKV1 * 129)].rearrange(
                              "p (h c) -> p h c", h=HKV1)
                kb.op("V", lambda e, va=va, pkv=pkv: e.tensor_copy(
                    va[:, :, 1:129], pkv[:, 512:1024].rearrange("p (h d) -> p h d", h=HKV1)),
                    reads=("p1kv",), writes=("v1aug",))
                kb.op("V", lambda e, va=va: e.memset(va[:, :, 0:1], 1.0),
                      writes=("v1aug",))

            def emit_front(t):
                par = t % 2
                kb.dma("Y", f"dx{par}", lambda e, t=t, par=par: e.dma_start(
                    out=self.xt(par)[:], in_=self.x1_hbm[t]),
                    reads=("x1hbm",), writes=(f"xt{par}",))
                kb.dma("Y", f"dtb{par}", lambda e, t=t, par=par: e.dma_start(
                    out=tab(par)[:, 0:128], in_=self.cosq_h[t * 128:(t + 1) * 128, :]),
                    writes=(f"tab{par}",))
                kb.dma("Y", f"dtb{par}", lambda e, t=t, par=par: e.dma_start(
                    out=tab(par)[:, 128:256], in_=self.sinq_h[t * 128:(t + 1) * 128, :]),
                    writes=(f"tab{par}b",))
                self.emit_ln(self.xt(par)[:], f"xt{par}", self.xn_f[:], "xn_f")
                kb.op("V", lambda e, par=par: e.tensor_copy(xn_b(par)[:], self.xn_f[:]),
                      reads=("xn_f",), writes=(f"xnb{par}",))

            def emit_rope(t):
                par = t % 2
                cos8 = tab(par)[:, 0:128].unsqueeze(1).broadcast_to((128, 8, 128))
                qv = qkf[:, 0:1024].rearrange("p (h w) -> p h w", h=8)
                qv4 = qkf[:, 0:1024].rearrange("p (h x w) -> p h x w", h=8, x=2)
                gb4 = rp_b[:].rearrange("p (h x w) -> p h x w", h=8, x=2)
                sinlo = tab(par)[:, 128:192].unsqueeze(1).broadcast_to((128, 8, 64))
                sinhi = tab(par)[:, 192:256].unsqueeze(1).broadcast_to((128, 8, 64))
                kb.op("V", lambda e, qv=qv, cos8=cos8: e.tensor_tensor(
                    rp_a[:].rearrange("p (h w) -> p h w", h=8), qv, cos8, ALU.mult),
                    reads=("qkvf", f"tab{par}"), writes=("rp_a",))
                kb.op("V", lambda e, gb4=gb4, qv4=qv4, sinlo=sinlo: e.tensor_tensor(
                    gb4[:, :, 0, :], qv4[:, :, 1, :], sinlo, ALU.mult),
                    reads=("qkvf", f"tab{par}b"), writes=("rp_b",))
                kb.op("V", lambda e, gb4=gb4, qv4=qv4, sinhi=sinhi: e.tensor_tensor(
                    gb4[:, :, 1, :], qv4[:, :, 0, :], sinhi, ALU.mult),
                    reads=("qkvf", f"tab{par}b"), writes=("rp_b",))
                kb.op("V", lambda e, par=par: e.tensor_tensor(
                    qr_b(par)[:], rp_a[:], rp_b[:], ALU.add),
                    reads=("rp_a", "rp_b"), writes=(f"qrb{par}",))
                kv4 = qkf[:, 1024:1536].rearrange("p (h x w) -> p h x w", h=4, x=2)
                gk4 = rp_b[:, 0:512].rearrange("p (h x w) -> p h x w", h=4, x=2)
                sinlo4 = tab(par)[:, 128:192].unsqueeze(1).broadcast_to((128, 4, 64))
                sinhi4 = tab(par)[:, 192:256].unsqueeze(1).broadcast_to((128, 4, 64))
                kcos = tab(par)[:, 0:128].unsqueeze(1).broadcast_to((128, 4, 128))
                kvv = qkf[:, 1024:1536].rearrange("p (h w) -> p h w", h=4)
                kb.op("V", lambda e, kvv=kvv, kcos=kcos: e.tensor_tensor(
                    rp_a[:, 0:512].rearrange("p (h w) -> p h w", h=4),
                    kvv, kcos, ALU.mult),
                    reads=("qkvf", f"tab{par}"), writes=("rp_a",))
                kb.op("V", lambda e, gk4=gk4, kv4=kv4, sinlo4=sinlo4: e.tensor_tensor(
                    gk4[:, :, 0, :], kv4[:, :, 1, :], sinlo4, ALU.mult),
                    reads=("qkvf", f"tab{par}b"), writes=("rp_b",))
                kb.op("V", lambda e, gk4=gk4, kv4=kv4, sinhi4=sinhi4: e.tensor_tensor(
                    gk4[:, :, 1, :], kv4[:, :, 0, :], sinhi4, ALU.mult),
                    reads=("qkvf", f"tab{par}b"), writes=("rp_b",))
                kb.op("V", lambda e, par=par: e.tensor_tensor(
                    kr_b(par)[:], rp_a[:, 0:512], rp_b[:, 0:512], ALU.add),
                    reads=("rp_a", "rp_b"), writes=(f"krb{par}",))

            def emit_mid(t):
                par = t % 2
                self.emit_transpose(xn_b(par), (f"xnb{par}",),
                                    lambda cb, par=par: sxn(par)[:, cb * 128:(cb + 1) * 128],
                                    f"sxn{par}", NC)
                pq, pkv = p_q[par], p_kv[par]
                for c in range(NC):
                    lhs = sxn(par)[:, c * 128:(c + 1) * 128]
                    for nh in range(2):
                        kb.op("T", lambda e, c=c, nh=nh, lhs=lhs, pq=pq: e.matmul(
                            pq[:, nh * 512:(nh + 1) * 512], lhs,
                            wq1(c)[:, nh * 512:(nh + 1) * 512],
                            start=(c == 0), stop=(c == NC - 1), skip_group_check=True),
                            reads=(f"sxn{par}", f"w1q{c}"), writes=(f"p1q{par}",))
                    kb.op("T", lambda e, c=c, lhs=lhs, pkv=pkv: e.matmul(
                        pkv[:, 0:512], lhs, wk1(c),
                        start=(c == 0), stop=(c == NC - 1), skip_group_check=True),
                        reads=(f"sxn{par}", f"w1k{c}"), writes=("p1kv",))
                    kb.op("T", lambda e, c=c, lhs=lhs, pkv=pkv: e.matmul(
                        pkv[:, 512:1024], lhs, wv1(c),
                        start=(c == 0), stop=(c == NC - 1), skip_group_check=True),
                        reads=(f"sxn{par}", f"w1v{c}"), writes=("p1kv",))

            def emit_quant(t):
                par = t % 2
                self.emit_round(self.xn_f[:], self.bcsb[:, 8 + 3:8 + 4], xi_b(par),
                                src_keys=("xn_f", "rs_bc"), dst_key=f"xib{par}", clip=127.0)

            def emit_back(t):
                par = t % 2
                self.emit_transpose(qr_b(par), (f"qrb{par}",),
                                    lambda cb, t=t: qsp[:, cb * 1024 + (t % 8) * 128:
                                                        cb * 1024 + (t % 8) * 128 + 128],
                                    "qsp", NC, evac_eng="S")
                if t % 8 == 7:
                    for h in range(H1):
                        kb.dma("G", "dqr", lambda e, h=h, sc=t // 8: e.dma_start(
                            out=self.qr1_hbm[h, sc], in_=qsp[:, h * 1024:(h + 1) * 1024]),
                            reads=("qsp",), writes=("qr1hbm",))
                self.emit_transpose(kr_b(par), (f"krb{par}",),
                                    lambda cb, t=t: ar2h[:, self.o_k1 + cb * S + t * 128:
                                                         self.o_k1 + cb * S + (t + 1) * 128],
                                    "k1f", 4, evac_eng="S")
                self.emit_transpose(xi_b(par), (f"xib{par}",),
                                    lambda cb, par=par: sti(par)[:, cb * 128:(cb + 1) * 128],
                                    f"sti{par}", NC, evac_eng="S")
                kb.dma("G", "dxint", lambda e, t=t, par=par: e.dma_start(
                    out=self.xint_hbm[t], in_=sti(par)[:]),
                    reads=(f"sti{par}",), writes=("xinthbm",))

            for t in range(NT):
                if t > 0:
                    emit_pre(t - 1)
                emit_front(t)
                emit_quant(t)
                if t > 0:
                    emit_rope(t - 1)
                emit_mid(t)
                if t > 0:
                    emit_back(t - 1)
            emit_pre(NT - 1)
            emit_rope(NT - 1)
            emit_back(NT - 1)
            kb.flush()
        ar2.pop()
        ar4.pop()

    def phaseA_l1(self):
        kb, ar2, nc = self.kb, self.ar2, self.nc
        ar2.push()
        ar2h = self.ar2h
        o_qb = ar2.take(2 * 1024)
        o_p = ar2.take(2 * 1024)
        o_sta = ar2.take(1024)
        o_stb = ar2.take(1024)
        qb = lambda p: ar2h[:, o_qb + p * 1024: o_qb + (p + 1) * 1024]
        pt = lambda p: ar2h[:, o_p + p * 1024: o_p + (p + 1) * 1024]
        stA = ar2h[:, o_sta: o_sta + 1024]
        stB = ar2h[:, o_stb: o_stb + 1024]
        k1 = lambda c: ar2h[:, self.o_k1 + c * S: self.o_k1 + (c + 1) * S]
        v1 = lambda t, h: ar2h[:, self.o_v1 + t * (HKV1 * 129) + h * 129:
                               self.o_v1 + t * (HKV1 * 129) + (h + 1) * 129]
        kb.start_phase()

        with ExitStack() as ps:
            psS = [ps.enter_context(nc.psum_tensor(f"pl_s{i}", [128, 1024], F32))
                   for i in range(2)]
            psO_A = ps.enter_context(nc.psum_tensor("pl_oa", [128, 1024], F32))
            psO_B = ps.enter_context(nc.psum_tensor("pl_ob", [128, 1024], F32))

            def emit_pv1(kv, j, jmax, qc):
                o = j - 8 * qc
                w0 = max(0, o * 128)
                segs = [(w0, 512), (512, 1024)] if w0 < 512 else [(w0, 1024)]
                pp = j % 2
                for (a, b) in segs:
                    kb.op("T", lambda e, j=j, a=a, b=b, kv=kv, pp=pp, jmax=jmax: e.matmul(
                        psO_A[0:65, a:b], v1(j, kv)[:, 0:65], pt(pp)[:, a:b],
                        start=(j == 0), stop=(j == jmax), skip_group_check=True),
                        reads=("v1aug", f"pt{pp}"), writes=("psOA1",))
                    kb.op("T", lambda e, j=j, a=a, b=b, kv=kv, pp=pp, jmax=jmax: e.matmul(
                        psO_B[0:64, a:b], v1(j, kv)[:, 65:129], pt(pp)[:, a:b],
                        start=(j == 0), stop=(j == jmax), skip_group_check=True),
                        reads=("v1aug", f"pt{pp}"), writes=("psOB1",))

            for h in range(H1):
                kv = h // 2
                for qc in range(4):
                    par = (h * 4 + qc) % 2
                    kb.dma("Y", f"dqb{par}", lambda e, h=h, qc=qc, par=par: e.dma_start(
                        out=qb(par), in_=self.qr1_hbm[h, qc]),
                        reads=("qr1hbm",), writes=(f"qb1_{par}",))
                    jmax = 8 * qc + 7
                    for j in range(jmax + 1):
                        o = j - 8 * qc
                        w0 = max(0, o * 128)
                        pp = j % 2
                        ps_ = psS[pp]
                        pkey = f"plS{pp}"
                        segs = [(w0, 512), (512, 1024)] if w0 < 512 else [(w0, 1024)]
                        for (a, b) in segs:
                            kb.op("T", lambda e, h=h, j=j, a=a, b=b, kv=kv, par=par, ps_=ps_: e.matmul(
                                ps_[:, a:b], k1(kv)[:, j * 128:(j + 1) * 128],
                                qb(par)[:, a:b],
                                start=True, stop=True, skip_group_check=True),
                                reads=("k1f", f"qb1_{par}"), writes=(pkey,))
                        kb.op("S", lambda e, w0=w0, ps_=ps_, pp=pp: e.activation(
                            pt(pp)[:, w0:1024], ps_[:, w0:1024], AF.Exp,
                            scale=self.consts[:, 2:3]),
                            reads=(pkey, "econ"), writes=(f"pt{pp}",))
                        if o >= 0:
                            kb.op("V", lambda e, w0=w0, pp=pp: e.tensor_tensor(
                                pt(pp)[:, w0:w0 + 128], pt(pp)[:, w0:w0 + 128],
                                self.triu[:], ALU.mult),
                                reads=(f"pt{pp}", "triu"), writes=(f"pt{pp}",))
                        if j > 0:
                            emit_pv1(kv, j - 1, jmax, qc)
                    emit_pv1(kv, jmax, jmax, qc)
                    kb.op("V", lambda e: e.reciprocal(self.rl[0:1, 0:1024], psO_A[0:1, :]),
                          reads=("psOA1",), writes=("rlX",))
                    kb.op("V", lambda e: e.tensor_copy(stA[0:65, :], psO_A[0:65, :]),
                          reads=("psOA1",), writes=("stA1",))
                    kb.op("V", lambda e: e.tensor_copy(stB[0:64, :], psO_B[0:64, :]),
                          reads=("psOB1",), writes=("stB1",))
                    for nh in range(2):
                        kb.op("T", lambda e, nh=nh: e.matmul(
                            psO_A[0:128, nh * 512:(nh + 1) * 512],
                            self.onesf_row[0:1, 0:128],
                            self.rl[0:1, nh * 512:(nh + 1) * 512],
                            start=True, stop=True, skip_group_check=True),
                            reads=("rlX", "onesr", "stA1"), writes=("psOA1",))
                    kb.op("V", lambda e: e.tensor_tensor(stA[0:65, :], stA[0:65, :],
                                                         psO_A[0:65, :], ALU.mult),
                          reads=("stA1", "psOA1"), writes=("stA1",))
                    kb.op("V", lambda e: e.tensor_tensor(stB[0:64, :], stB[0:64, :],
                                                         psO_A[0:64, :], ALU.mult),
                          reads=("stB1", "psOA1"), writes=("stB1",))
                    kb.dma("Y", "dstA", lambda e, h=h, qc=qc: e.dma_start(
                        out=self.attn1_hbm[h][0:64, qc * 1024:(qc + 1) * 1024],
                        in_=stA[1:65, :]), reads=("stA1",), writes=("attn1hbm",))
                    kb.dma("Y", "dstB", lambda e, h=h, qc=qc: e.dma_start(
                        out=self.attn1_hbm[h][64:128, qc * 1024:(qc + 1) * 1024],
                        in_=stB[0:64, :]), reads=("stB1",), writes=("attn1hbm",))
            kb.flush()
        ar2.pop()

    def phaseC_l1(self):
        kb, ar2, nc = self.kb, self.ar2, self.nc
        ar2.push()
        ar2h = self.ar2h
        o_w1 = ar2.take(NC * D)
        o_wo = ar2.take(NC * D)
        o_xit = ar2.take(2 * 1024)
        o_att = ar2.take(2 * 8192)
        w1 = lambda c: ar2h[:, o_w1 + c * D: o_w1 + (c + 1) * D]
        wo = lambda c: ar2h[:, o_wo + c * D: o_wo + (c + 1) * D]
        xit = lambda p: ar2h[:, o_xit + p * 1024: o_xit + (p + 1) * 1024]
        attsp = lambda p: ar2h[:, o_att + p * 8192: o_att + (p + 1) * 8192]
        kb.start_phase()

        for c in range(NC):
            kb.dma("G", "dwt", lambda e, c=c: e.dma_start(
                out=self.rnd_i[:, 0:1024], in_=self.ternT_h[c * 128:(c + 1) * 128, :]),
                reads=("rn_ci",), writes=("rn_ci",))
            kb.op("V", lambda e, c=c: e.tensor_scalar_add(w1(c), self.rnd_i[:, 0:1024], -1.0),
                  reads=("rn_ci",), writes=(f"tw1{c}",))
        for c in range(NC):
            kb.dma("G", "dwo1", lambda e, c=c: e.dma_start(
                out=wo(c), in_=self.wo1_h[c * 128:(c + 1) * 128, :]), writes=(f"wo1{c}",))

        with ExitStack() as ps:
            p_y = [ps.enter_context(nc.psum_tensor(f"py_{i}", [128, 1024], F32))
                   for i in range(2)]
            p_s = [ps.enter_context(nc.psum_tensor(f"ps1_{i}", [128, 1024], F32))
                   for i in range(2)]

            for t in range(NT):
                par = t % 2
                kb.dma("Y", f"dxi{par}", lambda e, t=t, par=par: e.dma_start(
                    out=xit(par), in_=self.xint_hbm[t]),
                    reads=("xinthbm",), writes=(f"xt1_{par}",))
                if t % 8 == 0:
                    sc = t // 8
                    kb.dma("Y", f"dat{sc % 2}", lambda e, sc=sc: e.dma_start(
                        out=attsp(sc % 2).rearrange("p (c w) -> p c w", c=NC),
                        in_=self.attn1_hbm[:, :, sc * 1024:(sc + 1) * 1024]
                            .transpose([1, 0, 2])),
                        reads=("attn1hbm",), writes=(f"at1_{sc % 2}",))
                kb.dma("Y", f"dx{par}", lambda e, t=t, par=par: e.dma_start(
                    out=self.xt(par)[:], in_=self.x1_hbm[t]),
                    reads=("x1hbm",), writes=(f"xt{par}",))
                py, psn = p_y[par], p_s[par]
                for nh in range(2):
                    kb.op("T", lambda e, nh=nh, py=py: e.matmul(
                        py[:, nh * 512:(nh + 1) * 512], self.onesb_row[0:1, :],
                        self.b1bf[0:1, nh * 512:(nh + 1) * 512],
                        start=True, stop=False, skip_group_check=True),
                        reads=("b1bf", "onesbr"), writes=(f"py{par}",))
                for c in range(NC):
                    lhs = xit(par)[:, c * 128:(c + 1) * 128]
                    for nh in range(2):
                        kb.op("T", lambda e, c=c, nh=nh, lhs=lhs, py=py: e.matmul(
                            py[:, nh * 512:(nh + 1) * 512], lhs,
                            w1(c)[:, nh * 512:(nh + 1) * 512],
                            start=False, stop=(c == NC - 1), skip_group_check=True),
                            reads=(f"xt1_{par}", f"tw1{c}"), writes=(f"py{par}",))
                    scp = (t // 8) % 2
                    lhsa = attsp(scp)[:, c * 1024 + (t % 8) * 128:
                                      c * 1024 + (t % 8) * 128 + 128]
                    for nh in range(2):
                        kb.op("T", lambda e, c=c, nh=nh, lhsa=lhsa, psn=psn: e.matmul(
                            psn[:, nh * 512:(nh + 1) * 512], lhsa,
                            wo(c)[:, nh * 512:(nh + 1) * 512],
                            start=(c == 0), stop=(c == NC - 1), skip_group_check=True),
                            reads=(f"at1_{scp}", f"wo1{c}"), writes=(f"ps1{par}",))
                kb.op("S", lambda e, py=py: e.activation(
                    self.ga_f[:], py[:], AF.Silu, scale=self.bcsb[:, 3:4]),
                    reads=(f"py{par}", "s_bc"), writes=("rn_a",))
                kb.op("V", lambda e, psn=psn: e.tensor_tensor(
                    self.gb_f[:], psn[:], self.ga_f[:], ALU.mult),
                    reads=(f"ps1{par}", "rn_a"), writes=("rn_b",))
                kb.op("V", lambda e, par=par: e.tensor_tensor(
                    self.xn_f[:], self.xt(par)[:], self.gb_f[:], ALU.add),
                    reads=(f"xt{par}", "rn_b"), writes=("xn_f",))
                kb.dma("Y", "dout", lambda e, t=t: e.dma_start(
                    out=self.y_h[t], in_=self.xn_f[:]),
                    reads=("xn_f",), writes=("yout",))
            kb.flush()
        ar2.pop()


_CACHE = {}


def _host_prep(inputs):
    f = np.ascontiguousarray
    x = np.asarray(inputs["L0_attn_x"], dtype=np.float32)
    t = np.arange(S, dtype=np.float64)
    inv = 1.0 / (10000.0 ** (np.arange(0, HD1, 2, dtype=np.float64) / HD1))
    fr = np.outer(t, inv)
    emb = np.concatenate([fr, fr], axis=1)
    cos = np.cos(emb).astype(np.float32)
    sin = np.sin(emb).astype(np.float32)
    sinp = sin.copy()
    sinp[:, :HD1 // 2] *= -1.0
    common = {
        "wqkvT": f(np.asarray(inputs["L0_attn_w_qkv"], np.float32).T),
        "woT": f(np.asarray(inputs["L0_attn_w_out"], np.float32).T),
        "wgT": f(np.asarray(inputs["L0_ffn_w_gate"], np.float32).T),
        "wuT": f(np.asarray(inputs["L0_ffn_w_up"], np.float32).T),
        "wdT": f(np.asarray(inputs["L0_ffn_w_down"], np.float32).T),
        "wq1": f(np.asarray(inputs["L1_attn_Wq"], np.float32)),
        "wk1": f(np.asarray(inputs["L1_attn_Wk"], np.float32)),
        "wv1": f(np.asarray(inputs["L1_attn_Wv"], np.float32)),
        "wo1": f(np.asarray(inputs["L1_attn_Wo"], np.float32)),
        "ternT": f(np.asarray(inputs["L1_ffn_ternary_weight"], np.int32).T),
        "bias1": f(np.asarray(inputs["L1_ffn_bias"], np.float32).reshape(1, D)),
        "ascale": f(np.asarray(inputs["L1_ffn_act_scale"], np.float32).reshape(1, 1)),
        "identbf": np.eye(128, dtype=ml_dtypes.bfloat16),
        "onesf_col": np.ones((128, 1), np.float32),
        "onesf_row": np.ones((1, 128), np.float32),
        "onesb_row": np.ones((1, 128), ml_dtypes.bfloat16),
        "triubf": np.triu(np.ones((128, 128))).astype(ml_dtypes.bfloat16),
        "cosq": f(cos),
        "sinq": f(sinp),
        "econsts": np.array([[L0_EXP_SCALE, L0_EXP_BIAS, L1_EXP_SCALE,
                              -0.5, 0.5, 0.0, 0.0, 0.0]], np.float32),
    }
    per_core = []
    for b in range(8):
        m = dict(common)
        m["x0"] = f(x[b].reshape(NT, 128, D))
        per_core.append(m)
    return per_core


def kernel(**inputs):
    per_core = _host_prep(inputs)
    if "nc" not in _CACHE:
        _CACHE["nc"] = Builder(DEBUG_TAPS).build()
    nc = _CACHE["nc"]
    res = run_bass_kernel_spmd(nc, per_core, core_ids=list(range(8)))
    out = np.empty((8, S, D), np.float32)
    for b in range(8):
        out[b] = np.asarray(res.results[b]["y"]).reshape(S, D)
    return out
